# version 23
# baseline (speedup 1.0000x reference)
from collections import deque
from contextlib import ExitStack

import numpy as np
import ml_dtypes

import concourse.bass as bass
import concourse.mybir as mybir
from concourse.bass_utils import run_bass_kernel_spmd
from concourse.aot_env import allow_ici_all_to_all

F32 = mybir.dt.float32
BF16 = mybir.dt.bfloat16
AF = mybir.ActivationFunctionType
ALU = mybir.AluOpType

NCORE = 8
S = 8192
D = 2048
DH = 128
FF = 8192
TOWN = 1024
NTC = 16
TANH_SCALE = float((128.0 ** -0.5) / 50.0)
EXP_SCALE = 50.0
FWHT_SCALE = float(2.0 ** -6.5)
EPS = 1e-5

DEBUG = False
USE_B_PRELOAD = False

ENGS = ("tensor", "scalar", "vector", "gpsimd", "sync")


class Sched:
    def __init__(self):
        self.q = {e: [] for e in ENGS}
        self.cnt = {}

    def op(self, eng, fn, waits=(), inc=None):
        if inc is not None:
            s, amt = inc
            self.cnt[s] = self.cnt.get(s, 0) + amt
        self.q[eng].append((fn, tuple(waits), inc))
        return self.cnt[inc[0]] if inc is not None else None

    def at(self, s):
        return self.cnt.get(s, 0)


def _hadamard(n):
    h = np.array([[1.0]], np.float32)
    while h.shape[0] < n:
        h = np.block([[h, h], [h, -h]])
    return h


def build_consts():
    c = np.zeros((128, 2688), np.float32)
    c[:, 0:128] = np.eye(128)
    c[:, 128:256] = _hadamard(128)
    c[:, 256:384] = np.kron(np.eye(2), _hadamard(64))
    c[:, 384:512] = 1.0
    ii = np.arange(512)[None, :]
    pp = np.arange(128)[:, None]
    for jj in range(4):
        c[:, 512 + jj * 512: 1024 + jj * 512] = (jj * 128 + pp <= ii).astype(np.float32)
    c[:, 2560:2561] = 1.0
    c[:, 2561:2562] = 2.0
    return c.astype(ml_dtypes.bfloat16)


def build_nc(debug=DEBUG):
    nc = bass.Bass("TRN2", target_bir_lowering=False, debug=False)

    x_in = nc.dram_tensor("x", [TOWN, D], F32, kind="ExternalInput").ap()
    wqkv = nc.dram_tensor("wqkv", [D, 768], F32, kind="ExternalInput").ap()
    wg = nc.dram_tensor("wg", [D, 2], F32, kind="ExternalInput").ap()
    wout = nc.dram_tensor("wout", [D, D], F32, kind="ExternalInput").ap()
    w1t = nc.dram_tensor("w1t", [D, FF], F32, kind="ExternalInput").ap()
    w3t = nc.dram_tensor("w3t", [D, FF], F32, kind="ExternalInput").ap()
    w2tp = nc.dram_tensor("w2tp", [FF, D], F32, kind="ExternalInput").ap()
    cst_in = nc.dram_tensor("cst", [128, 2688], BF16, kind="ExternalInput").ap()
    fc_in = nc.dram_tensor("fc", [128, 4], F32, kind="ExternalInput").ap()
    out = nc.dram_tensor("out", [TOWN, D], F32, kind="ExternalOutput").ap()

    ag1_in = nc.dram_tensor("ag1_in", [D, TOWN], BF16)
    ag1_out = nc.dram_tensor("ag1_out", [NCORE * D, TOWN], BF16, addr_space="Shared")
    a2a_in = nc.dram_tensor("a2a_in", [2048, TOWN], BF16)
    a2a_out = nc.dram_tensor("a2a_out", [2048, TOWN], BF16)
    z_dram = nc.dram_tensor("z_dram", [128, 64, TOWN], BF16)
    w2p_dram = nc.dram_tensor("w2p_dram", [FF, D], BF16)
    g_dram = nc.dram_tensor("g_dram", [2, S], F32)

    dbg = {}
    if debug:
        for name, shape, dt in [
            ("dbg_xnt", [128, 16 * TOWN], BF16),
            ("dbg_kt", [128, 2 * S], BF16),
            ("dbg_vt", [128, 2 * S], BF16),
            ("dbg_g", [2, S], F32),
            ("dbg_z", [128, 64 * TOWN], BF16),
            ("dbg_w2p", [FF, D], BF16),
            ("dbg_a2a", [2048, TOWN], BF16),
        ]:
            dbg[name] = nc.dram_tensor(name, shape, dt, kind="ExternalOutput").ap()

    sch = Sched()

    with ExitStack() as st:
        sb = lambda n, s_, d: st.enter_context(nc.sbuf_tensor(n, s_, d))
        psT = lambda n, s_, d: st.enter_context(nc.psum_tensor(n, s_, d))

        cst = sb("cst_sb", [128, 2688], BF16)
        fcb = sb("fcb", [128, 4], F32)
        A1 = sb("A1", [128, 16384], BF16)
        A2 = sb("A2", [128, 12288], BF16)
        A3 = sb("A3", [128, 16384], BF16)
        A4 = sb("A4", [128, 12288], BF16)
        A5 = sb("A5", [128, 12800], BF16)
        KT = sb("KT", [128, 16384], BF16)
        qsm = sb("qsm", [128, 2048], BF16)
        AFl = sb("AF", [128, 2048], F32)
        vtmp = sb("vtmp", [128, 1024], BF16)
        wgb = sb("wgb", [128, 32], BF16)
        bn24 = sb("bn24", [128, 24], F32)
        mv = sb("mv", [128, 2], F32)
        nm = sb("nm", [128, 1], F32)
        sq = sb("sq", [128, 1], F32)
        rs = sb("rs", [128, 1], F32)
        g2sb = sb("g2sb", [2, 512], F32)
        g_sb = sb("g_sb", [1, 512], F32)
        rd_sb = sb("rd_sb", [1, 512], F32)
        scs = sb("scs", [1, 512], BF16)
        bcs = sb("bcs", [128, 512], BF16)
        dacc = sb("dacc", [128, 512], F32)
        dbc = sb("dbc", [128, 512], BF16)
        gt_sb = sb("gt_sb", [1, 512], F32)

        ps_s = psT("ps_s", [128, 1024], F32)
        ps2 = psT("ps2", [128, 1024], F32)
        pbA = psT("pbA", [128, 512], F32)
        pbB = psT("pbB", [128, 512], F32)
        pTa = psT("pTa", [128, 1024], BF16)
        pTb = psT("pTb", [128, 1024], BF16)

        sem_names = (
            ["ten", "sca", "vec", "cc1", "cc2", "dmisc", "dgw", "dagi",
             "dxf0", "dxf1", "dw0", "dw1", "dz0", "dz1",
             "dxa0", "dxa1", "dgd", "dg4", "dog0", "dog1",
             "dwt0", "dwt1", "dwp0", "dwp1", "dac0", "dac1", "dxcp"]
            + [f"dz2{i}" for i in range(8)] + [f"dw2{i}" for i in range(8)]
            + [f"dwo{i}" for i in range(8)] + [f"dog4{i}" for i in range(8)]
        )
        sems = {n: st.enter_context(nc.semaphore(n)) for n in sem_names}

        W = sch.op
        at = sch.at

        pset = lambda i: ps_s if i % 2 == 0 else ps2

        xcp_done = W("sync", lambda e: e.dma_start(out=out, in_=x_in), inc=("dxcp", 16))
        cst_done = W("sync", lambda e: e.dma_start(out=cst[:, :], in_=cst_in), inc=("dmisc", 16))
        fc_done = W("sync", lambda e: e.dma_start(out=fcb[:, :], in_=fc_in), inc=("dmisc", 16))
        wqkvb = A4
        W("gpsimd", lambda e: e.dma_start(
            out=wqkvb[:, 0:12288].rearrange("p (dc m) -> p dc m", dc=16),
            in_=wqkv.rearrange("(dc p) m -> p dc m", p=128)), inc=("dgw", 16))
        wg_done = W("gpsimd", lambda e: e.dma_start(
            out=wgb[:, :].rearrange("p (dc g) -> p dc g", dc=16),
            in_=wg.rearrange("(dc p) g -> p dc g", p=128)), inc=("dgw", 16))

        xnb = KT
        xnT = A1
        apply_done = [0] * 8
        prev_sqv = 0
        prev_bnv = 0
        for tt in range(8):
            slot = tt % 2
            waits = []
            if tt >= 1:
                waits.append(("vec", apply_done[tt - 1]))
            dxf = W("sync", lambda e, tt=tt: e.dma_start(
                out=AFl[:, 0:2048],
                in_=x_in[tt * 128:(tt + 1) * 128, :]),
                waits=waits, inc=(f"dxf{slot}", 16))
            xfv = lambda: AFl[:, 0:2048]
            for i in range(4):
                w_ = []
                if i == 0:
                    w_.append((f"dxf{slot}", dxf))
                    if tt > 0:
                        w_.append(("vec", prev_bnv))
                W("vector", lambda e, i=i: e.bn_stats(
                    bn24[:, 6 * i:6 * i + 6], xfv()[:, 512 * i:512 * i + 512]),
                    waits=w_, inc=("vec", 1))
            bw = [("vec", at("vec"))]
            if tt > 0:
                bw.append(("sca", prev_sqv))
            bnv = W("vector", lambda e: e.bn_aggr(mv[:, :], bn24[:, :]),
                    waits=bw, inc=("vec", 1))
            prev_bnv = bnv
            nmv = W("vector", lambda e: e.tensor_scalar(
                nm[:, :], mv[:, 0:1], fcb[:, 1:2], None, op0=ALU.mult),
                waits=[("vec", bnv), ("dmisc", fc_done)], inc=("vec", 1))
            sqv = W("scalar", lambda e: e.activation(sq[:, :], mv[:, 1:2], AF.Sqrt,
                                                     bias=fcb[:, 0:1]),
                    waits=[("vec", bnv), ("dmisc", fc_done)], inc=("sca", 1))
            prev_sqv = sqv
            rsv = W("vector", lambda e: e.reciprocal(rs[:, :], sq[:, :]),
                    waits=[("sca", sqv), ("vec", nmv)], inc=("vec", 1))
            apply_done[tt] = W("vector", lambda e, tt=tt: e.tensor_scalar(
                xnb[:, tt * 2048:(tt + 1) * 2048], xfv()[:, :], nm[:, 0:1], rs[:, 0:1],
                op0=ALU.add, op1=ALU.mult),
                waits=[("vec", rsv)], inc=("vec", 1))

        ln_all = apply_done[7]
        tp_done = [0, 0]
        ev_done = [0, 0]
        for dc in range(16):
            slot = dc % 2
            pT = pTa if slot == 0 else pTb
            for tt in range(8):
                w_ = []
                if dc == 0 and tt == 0:
                    w_ = [("vec", ln_all), ("dmisc", cst_done)]
                if tt == 0 and dc >= 2:
                    w_.append(("vec", ev_done[slot]))
                r_ = W("tensor", lambda e, dc=dc, tt=tt, pT=pT: e.transpose(
                    pT[:, tt * 128:(tt + 1) * 128],
                    xnb[:, tt * 2048 + dc * 128: tt * 2048 + dc * 128 + 128],
                    cst[:, 0:128]),
                    waits=w_, inc=("ten", 1) if tt == 7 else None)
                if tt == 7:
                    tp_done[slot] = r_
            ev_done[slot] = W("vector", lambda e, dc=dc, pT=pT: e.tensor_copy(
                xnT[:, dc * 1024:(dc + 1) * 1024], pT[:, :]),
                waits=[("ten", tp_done[slot])], inc=("vec", 1))
        xnt_ready_all = max(ev_done)

        agi = 0
        for dc in range(16):
            agi = W("sync", lambda e, dc=dc: e.dma_start(
                out=ag1_in[dc * 128:(dc + 1) * 128, :],
                in_=xnT[:, dc * 1024:(dc + 1) * 1024]),
                waits=[("vec", ev_done[dc % 2])], inc=("dagi", 16))
        W("gpsimd", lambda e: e.collective_compute(
            "AllGather", ALU.bypass, replica_groups=[list(range(NCORE))],
            ins=[ag1_in.ap().opt()], outs=[ag1_out.ap().opt()]),
            waits=[("dagi", agi)], inc=("cc1", 1))

        if debug:
            W("sync", lambda e: e.dma_start(out=dbg["dbg_xnt"], in_=xnT[:, :]),
              waits=[("vec", xnt_ready_all)], inc=("dmisc", 16))

        w13 = lambda slot: A2[:, slot * 4096:(slot + 1) * 4096]
        hbuf = lambda slot: A2[:, 8192 + slot * 512: 8192 + (slot + 1) * 512]
        stmp = lambda slot: A2[:, 9216 + slot * 512: 9216 + (slot + 1) * 512]
        zstage = lambda slot: A2[:, 10240 + slot * 512: 10240 + (slot + 1) * 512]
        s1b = lambda slot: A2[:, 11264 + slot * 512: 11264 + (slot + 1) * 512]

        kt, vt = KT, A3
        xadst = lambda slot: A5[:, 0:8192]
        tts = A5[:, 8192:9216]
        pts = lambda slot: A5[:, 9216 + slot * 1024: 9216 + (slot + 1) * 1024]
        og_stage = lambda slot: A5[:, 11264 + slot * 512: 11264 + (slot + 1) * 512]

        ps2_guard = [[], []]
        pbB_guard = []

        def guard_waits(g):
            w = list(g)
            del g[:]
            return w

        NSTEP = 128
        upmm_done = [0] * NSTEP
        tsil_done = [0] * NSTEP
        s1_done = [0] * NSTEP
        hm_done = [0] * NSTEP
        zmm_done = [0] * NSTEP
        zev_done = [0] * NSTEP
        zdma_done = [0] * NSTEP
        w13_done = [0] * 64

        def z_unit(sig):
            if sig < 0 or sig >= NSTEP:
                return
            chi, tau = divmod(sig, 2)
            hslot = sig % 2
            waits = [("vec", hm_done[sig])] + guard_waits(pbB_guard)
            zmm_done[sig] = W("tensor", lambda e, hslot=hslot: e.matmul(
                pbB[:, :], cst[:, 128:256], hbuf(hslot),
                start=True, stop=True, skip_group_check=True),
                waits=waits, inc=("ten", 1))
            zw = [("ten", zmm_done[sig])]
            if sig >= 2:
                zw.append((f"dz{hslot}", zdma_done[sig - 2]))
            zev_done[sig] = W("vector", lambda e, hslot=hslot: e.tensor_copy(
                zstage(hslot), pbB[:, :]),
                waits=zw, inc=("vec", 1))
            pbB_guard.append(("vec", zev_done[sig]))
            zdma_done[sig] = W("sync", lambda e, chi=chi, tau=tau, hslot=hslot: e.dma_start(
                out=z_dram[:, chi, tau * 512:(tau + 1) * 512], in_=zstage(hslot)),
                waits=[("vec", zev_done[sig])], inc=(f"dz{hslot}", 16))

        def emit_w13_load(chi):
            wslot = chi % 2
            w_ = []
            if chi >= 2:
                w_.append(("ten", upmm_done[2 * (chi - 2) + 1]))
            for wi, wt in ((0, w1t), (1, w3t)):
                w13_done[chi] = W("gpsimd", lambda e, chi=chi, wi=wi, wt=wt, wslot=wslot: e.dma_start(
                    out=w13(wslot)[:, wi * 2048:(wi + 1) * 2048].rearrange("p (dc f) -> p dc f", dc=16),
                    in_=wt[:, chi * 128:(chi + 1) * 128].rearrange("(dc p) f -> p dc f", p=128)),
                    waits=w_ if wi == 0 else [], inc=(f"dw{wslot}", 16))

        def p2_chunks(sig):
            chi, tau = divmod(sig, 2)
            wslot = chi % 2
            hslot = sig % 2

            def up_half(wi, half):
                ww = []
                if half == 0:
                    ww = guard_waits(ps2_guard[wi])
                    if wi == 0:
                        ww.append((f"dw{wslot}", w13_done[chi]))
                        if sig == 0:
                            ww.append(("vec", xnt_ready_all))
                for dc in range(half * 8, half * 8 + 8):
                    W("tensor", lambda e, wi=wi, dc=dc: e.matmul(
                        ps2[:, wi * 512:(wi + 1) * 512],
                        w13(wslot)[:, wi * 2048 + dc * 128: wi * 2048 + dc * 128 + 128],
                        xnT[:, dc * 1024 + tau * 512: dc * 1024 + tau * 512 + 512],
                        start=(dc == 0), stop=(dc == 15), skip_group_check=True),
                        waits=ww if dc == half * 8 else [],
                        inc=("ten", 1) if dc == 15 else None)

            def c_wi0b():
                up_half(0, 1)
                upg = at("ten")
                tsw = [("ten", upg)]
                if sig >= 2:
                    tsw.append(("vec", s1_done[sig - 2]))
                tsil_done[sig] = W("scalar", lambda e: e.activation(
                    stmp(hslot), ps2[:, 0:512], AF.Tanh, scale=0.5),
                    waits=tsw, inc=("sca", 1))
                s1w = [("sca", tsil_done[sig]), ("dmisc", fc_done)]
                if sig >= 2:
                    s1w.append(("ten", zmm_done[sig - 2]))
                s1_done[sig] = W("vector", lambda e: e.scalar_tensor_tensor(
                    s1b(hslot), stmp(hslot), fcb[:, 3:4], ps2[:, 0:512],
                    op0=ALU.add, op1=ALU.mult),
                    waits=s1w, inc=("vec", 1))
                ps2_guard[0].extend([("sca", tsil_done[sig]), ("vec", s1_done[sig])])

            def c_wi1b():
                up_half(1, 1)
                upmm_done[sig] = at("ten")
                hw = [("vec", s1_done[sig]), ("ten", upmm_done[sig])]
                if sig >= 2:
                    hw.append(("ten", zmm_done[sig - 2]))
                hm_done[sig] = W("vector", lambda e: e.tensor_tensor(
                    hbuf(hslot), s1b(hslot), ps2[:, 512:1024], op=ALU.mult),
                    waits=hw, inc=("vec", 1))
                ps2_guard[1].append(("vec", hm_done[sig]))
                if tau == 1 and chi + 2 < 64:
                    emit_w13_load(chi + 2)

            return [(1, lambda: z_unit(sig - 2)),
                    (8, lambda: up_half(0, 0)), (8, c_wi0b),
                    (8, lambda: up_half(1, 0)), (8, c_wi1b)]

        m_evac_done = [[0] * 6 for _ in range(NTC)]
        g_done = [0] * NTC
        gev_done = [0] * NTC
        vtr_done = [0] * NTC
        vte_done = [0] * NTC
        xad_cnt = [0] * NTC
        gdma = 0

        def p3_chunks(tc, u):
            slot = tc % 2

            if u == 0:
                def c_dma():
                    r, wnd = divmod(tc, 2)
                    w_ = [("cc1", 1)]
                    if tc >= 1:
                        w_.append(("ten", g_done[tc - 1]))
                    xad_cnt[tc] = W("sync", lambda e, r=r, wnd=wnd: e.dma_start(
                        out=xadst(slot).rearrange("p (dc t) -> p dc t", dc=16),
                        in_=ag1_out[r * 2048:(r + 1) * 2048, wnd * 512:(wnd + 1) * 512]
                        .rearrange("(dc p) t -> p dc t", p=128)),
                        waits=w_, inc=(f"dxa{slot}", 16))
                return [(0, c_dma)]

            if 1 <= u <= 6:
                m = u - 1
                half = m % 2
                pdst = ps2[:, half * 512:(half + 1) * 512]

                def m_half(hh):
                    ww = []
                    if hh == 0:
                        ww = [(f"dxa{slot}", xad_cnt[tc])] + guard_waits(ps2_guard[half])
                        if tc == 0 and m == 0:
                            ww.append(("dgw", wg_done))
                    for dc in range(hh * 8, hh * 8 + 8):
                        W("tensor", lambda e, dc=dc: e.matmul(
                            pdst, wqkvb[:, dc * 768 + m * 128: dc * 768 + m * 128 + 128],
                            xadst(slot)[:, dc * 512:(dc + 1) * 512],
                            start=(dc == 0), stop=(dc == 15), skip_group_check=True),
                            waits=ww if dc == hh * 8 else [],
                            inc=("ten", 1) if dc == 15 else None)

                def c_mb():
                    m_half(1)
                    last = at("ten")
                    if m < 2:
                        dview = (lambda s_=m, tc=tc:
                                 qsm[:, (tc % 2) * 1024 + s_ * 512: (tc % 2) * 1024 + s_ * 512 + 512])
                    elif m < 4:
                        dview = (lambda s_=m - 2, tc=tc:
                                 kt[:, s_ * S + tc * 512: s_ * S + tc * 512 + 512])
                    else:
                        dview = (lambda s_=m - 4: vtmp[:, s_ * 512:(s_ + 1) * 512])
                    ew = [("ten", last)]
                    if tc == 0 and m == 0:
                        ew.append(("dagi", agi))
                    if m < 2 and tc >= 2:
                        ew.append(("ten", last_smm_of_j[tc - 2]))
                    if m >= 4 and tc >= 1:
                        ew.append(("ten", vtr_done[tc - 1]))
                    m_evac_done[tc][m] = W("vector", lambda e, dview=dview:
                                           e.tensor_copy(dview(), pdst),
                                           waits=ew, inc=("vec", 1))
                    ps2_guard[half].append(("vec", m_evac_done[tc][m]))
                return [(8, lambda: m_half(0)), (8, c_mb)]

            if u == 7:
                def g_half(hh):
                    ww = []
                    if hh == 0:
                        ww = [(f"dxa{slot}", xad_cnt[tc])] + guard_waits(ps2_guard[0])
                    for dc in range(hh * 8, hh * 8 + 8):
                        W("tensor", lambda e, dc=dc: e.matmul(
                            ps2[0:2, 0:512], wgb[:, dc * 2:(dc + 1) * 2],
                            xadst(slot)[:, dc * 512:(dc + 1) * 512],
                            start=(dc == 0), stop=(dc == 15), skip_group_check=True),
                            waits=ww if dc == hh * 8 else [],
                            inc=("ten", 1) if dc == 15 else None)

                def c_gb():
                    g_half(1)
                    g_done[tc] = at("ten")
                    gv = [("ten", g_done[tc])]
                    if tc >= 1:
                        gv.append(("dgd", sch.at("dgd")))
                    gev_done[tc] = W("vector", lambda e: e.tensor_copy(g2sb[:, :], ps2[0:2, 0:512]),
                                     waits=gv, inc=("vec", 1))
                    ps2_guard[0].append(("vec", gev_done[tc]))
                    for s_ in range(2):
                        W("gpsimd", lambda e, s_=s_: e.dma_start(
                            out=g_dram[s_:s_ + 1, tc * 512:(tc + 1) * 512], in_=g2sb[s_:s_ + 1, :]),
                            waits=[("vec", gev_done[tc])] if s_ == 0 else [], inc=("dgd", 16))
                return [(8, lambda: g_half(0)), (8, c_gb)]

            def c_vt():
                vw = [("vec", m_evac_done[tc][4]), ("vec", m_evac_done[tc][5])]
                if tc >= 1:
                    vw.append(("vec", vte_done[tc - 1]))
                for s_ in range(2):
                    for qq in range(4):
                        W("tensor", lambda e, s_=s_, qq=qq: e.transpose(
                            pTa[:, (s_ * 4 + qq) * 128:(s_ * 4 + qq) * 128 + 128],
                            vtmp[:, s_ * 512 + qq * 128: s_ * 512 + qq * 128 + 128],
                            cst[:, 0:128]),
                            waits=vw if (s_ == 0 and qq == 0) else [],
                            inc=("ten", 1) if (s_ == 1 and qq == 3) else None)
                vtr_done[tc] = at("ten")
                v1 = W("vector", lambda e, tc=tc: e.tensor_copy(
                    vt[:, tc * 512: tc * 512 + 512], pTa[:, 0:512]),
                    waits=[("ten", vtr_done[tc])], inc=("vec", 1))
                vte_done[tc] = W("vector", lambda e, tc=tc: e.tensor_copy(
                    vt[:, S + tc * 512: S + tc * 512 + 512], pTa[:, 512:1024]),
                    waits=[("vec", v1)], inc=("vec", 1))
            return [(3, c_vt)]

        pair_list = []
        pairs_by_j = {}
        gi = 0
        for qi in range(NTC):
            for s_ in range(2):
                npairs = 2 * (qi + 1)
                for p in range(npairs):
                    pairs_by_j.setdefault(qi, []).append(len(pair_list))
                    pair_list.append((gi, s_, qi, p, npairs))
                gi += 1

        g4_done = 0
        scs_done = 0
        last_smm_of_j = [0] * NTC
        tanh_done = {}
        exp_done = {}
        av_read = {}
        dacc_done = {}
        last_ogm = 0
        last_ogdma = [0, 0]
        last_rdv = 0
        last_bcev = 0
        last_bcmm = 0
        last_onesmm = 0
        last_dbc = 0
        last_gt = 0
        first_p4 = True

        def emit_av(gp):
            gi, s_, qi, p, npairs = pair_list[gp]
            pslot = gp % 2
            diag = (p >= npairs - 2)
            mw = [("vec", dacc_done[gp])] if diag else [("sca", exp_done[gp])]
            if p == 0 and last_ogm:
                mw.append(("vec", last_ogm))
            for half in range(2):
                jb = 2 * p + half
                W("tensor", lambda e, s_=s_, jb=jb, pslot=pslot, half=half, p=p,
                    npairs=npairs: e.matmul(
                    pbA[:, :], vt[:, s_ * S + jb * 128: s_ * S + jb * 128 + 128],
                    pts(pslot)[:, half * 512:(half + 1) * 512],
                    start=(p == 0 and half == 0), stop=(p == npairs - 1 and half == 1),
                    skip_group_check=True),
                    waits=mw if half == 0 else [], inc=("ten", 1) if half == 1 else None)
            av_read[gp] = at("ten")

        def emit_epilogue(gp, pull=None):
            nonlocal scs_done, last_rdv, last_bcev, last_bcmm, last_ogm
            nonlocal last_onesmm, last_dbc, last_gt
            gi, s_, qi, p, npairs = pair_list[gp]
            dw = [("vec", dacc_done[gp])]
            if last_onesmm:
                dw.append(("ten", last_onesmm))
            last_dbc = W("vector", lambda e: e.tensor_copy(dbc[:, :], dacc[:, :]),
                         waits=dw, inc=("vec", 1))
            if pull:
                pull(3)
            omw = [("vec", last_dbc)] + guard_waits(pbB_guard)
            last_onesmm = W("tensor", lambda e: e.matmul(
                pbB[0:1, :], cst[:, 2561:2562], dbc[:, :],
                start=True, stop=True, skip_group_check=True),
                waits=omw, inc=("ten", 1))
            gtw = [("dg4", g4_done)]
            last_gt = W("scalar", lambda e: e.activation(gt_sb[:, :], g_sb[:, :],
                                                         AF.Tanh, scale=0.5),
                        waits=gtw, inc=("sca", 1))
            rw = [("ten", last_onesmm)]
            last_rdv = W("vector", lambda e: e.reciprocal(rd_sb[:, :], pbB[0:1, :]),
                         waits=rw, inc=("vec", 1))
            scw = [("vec", last_rdv), ("sca", last_gt)]
            if last_bcmm:
                scw.append(("ten", last_bcmm))
            scs_done = W("vector", lambda e: e.scalar_tensor_tensor(
                scs[:, :], gt_sb[:, :], fcb[0:1, 3:4], rd_sb[:, :],
                op0=ALU.add, op1=ALU.mult),
                waits=scw, inc=("vec", 1))
            if pull:
                pull(3)
            last_bcmm = W("tensor", lambda e: e.matmul(
                pbB[:, :], cst[0:1, 384:512], scs[:, :],
                start=True, stop=True, skip_group_check=True),
                waits=[("vec", scs_done)], inc=("ten", 1))
            last_bcev = W("vector", lambda e: e.tensor_copy(bcs[:, :], pbB[:, :]),
                          waits=[("ten", last_bcmm)], inc=("vec", 1))
            pbB_guard.append(("vec", last_bcev))
            ogslot = gi % 2
            ogw = [("vec", last_bcev), ("ten", av_read[gp])]
            if last_ogdma[ogslot]:
                ogw.append((f"dog{ogslot}", last_ogdma[ogslot]))
            last_ogm = W("vector", lambda e, ogslot=ogslot: e.tensor_tensor(
                og_stage(ogslot), pbA[:, :], bcs[:, :], op=ALU.mult),
                waits=ogw, inc=("vec", 1))
            jdst, wdst = divmod(qi, 2)
            last_ogdma[ogslot] = W("sync", lambda e, jdst=jdst, wdst=wdst, s_=s_,
                                   ogslot=ogslot: e.dma_start(
                out=a2a_in[jdst * 256 + s_ * 128: jdst * 256 + s_ * 128 + 128,
                           wdst * 512:(wdst + 1) * 512],
                in_=og_stage(ogslot)),
                waits=[("vec", last_ogm)], inc=(f"dog{ogslot}", 16))

        def emit_pair(gp, pull=None):
            nonlocal first_p4, g4_done
            gi, s_, qi, p, npairs = pair_list[gp]
            pslot = gp % 2
            diag = (p >= npairs - 2)
            sw = []
            if first_p4:
                sw = [("dmisc", cst_done)]
                first_p4 = False
            if gp >= 1:
                sw.append(("sca", tanh_done[gp - 1]))
            if p == 0:
                sw.append(("vec", max(m_evac_done[qi])))
                sw.append(("vec", vte_done[qi]))
            for half in range(2):
                jb = 2 * p + half
                W("tensor", lambda e, s_=s_, qi=qi, jb=jb, half=half: e.matmul(
                    ps_s[:, half * 512:(half + 1) * 512],
                    kt[:, s_ * S + jb * 128: s_ * S + jb * 128 + 128],
                    qsm[:, (qi % 2) * 1024 + s_ * 512: (qi % 2) * 1024 + s_ * 512 + 512],
                    start=True, stop=True, skip_group_check=True),
                    waits=sw if half == 0 else [],
                    inc=("ten", 1) if half == 1 else None)
            smm = at("ten")
            last_smm_of_j[qi] = smm
            if pull:
                pull(2)
            if gp >= 1:
                emit_av(gp - 1)
                pgi, _, _, pp, pnp = pair_list[gp - 1]
                if pp == pnp - 1:
                    emit_epilogue(gp - 1, pull)
            if p == 0:
                gw = [("dgd", at("dgd"))] if gp == 0 else [("vec", scs_done)]
                g4_done = W("sync", lambda e, s_=s_, qi=qi: e.dma_start(
                    out=g_sb[:, :], in_=g_dram[s_:s_ + 1, qi * 512:(qi + 1) * 512]),
                    waits=gw, inc=("dg4", 16))
            tw = [("ten", smm)]
            if gp >= 1:
                tw.append(("sca", exp_done[gp - 1]))
            tanh_done[gp] = W("scalar", lambda e: e.activation(
                tts[:, :], ps_s[:, :], AF.Tanh, scale=TANH_SCALE),
                waits=tw, inc=("sca", 1))
            ew = [("sca", tanh_done[gp])]
            if gp >= 2:
                ew.append(("ten", av_read[gp - 2]))
                ew.append(("vec", dacc_done[gp - 2]))
            exp_done[gp] = W("scalar", lambda e, pslot=pslot: e.activation(
                pts(pslot), tts[:, :], AF.Exp, scale=EXP_SCALE),
                waits=ew, inc=("sca", 1))
            if diag:
                for half in range(2):
                    jj = 2 * (p - (npairs - 2)) + half
                    W("vector", lambda e, pslot=pslot, half=half, jj=jj: e.tensor_tensor(
                        pts(pslot)[:, half * 512:(half + 1) * 512],
                        pts(pslot)[:, half * 512:(half + 1) * 512],
                        cst[:, 512 + jj * 512: 1024 + jj * 512], op=ALU.mult),
                        waits=([("sca", exp_done[gp])] if half == 0 else [("vec", at("vec"))]),
                        inc=("vec", 1))
            dv = [("vec", at("vec"))]
            if not diag:
                dv.append(("sca", exp_done[gp]))
            if p == 0:
                if last_dbc:
                    dv.append(("vec", last_dbc))
                dacc_done[gp] = W("vector", lambda e, pslot=pslot: e.tensor_copy(
                    dacc[:, :], pts(pslot)[:, 0:512]),
                    waits=dv, inc=("vec", 1))
            else:
                dacc_done[gp] = W("vector", lambda e, pslot=pslot: e.tensor_tensor(
                    dacc[:, :], dacc[:, :], pts(pslot)[:, 0:512], op=ALU.add),
                    waits=dv, inc=("vec", 1))
            dacc_done[gp] = W("vector", lambda e, pslot=pslot: e.tensor_tensor(
                dacc[:, :], dacc[:, :], pts(pslot)[:, 512:1024], op=ALU.add),
                waits=[("vec", dacc_done[gp])], inc=("vec", 1))

        tsb = lambda slot: A5[:, slot * 2048:(slot + 1) * 2048]
        wps = lambda slot: A5[:, 4096 + slot * 2048: 4096 + (slot + 1) * 2048]
        tf_load_done = [0, 0]
        tf_wps_dma = [0, 0]
        tf_mm_done_k = {}
        tf_wp_count = {}
        tf_ev = {}

        def tf_chunks(kt_):
            slot = kt_ % 2

            def c_load():
                lw = []
                if kt_ < 2:
                    lw.append(("ten", g_done[15]))
                else:
                    lw.append(("ten", tf_mm_done_k[kt_ - 2]))
                tf_load_done[slot] = W("gpsimd", lambda e: e.dma_start(
                    out=tsb(slot), in_=w2tp[kt_ * 128:(kt_ + 1) * 128, :]),
                    waits=lw, inc=(f"dwt{slot}", 16))

            def c_mm(j0):
                for j in (j0, j0 + 1):
                    h = j % 2
                    mw = guard_waits(ps2_guard[h])
                    if j == 0:
                        mw.append((f"dwt{slot}", tf_load_done[slot]))
                    mm = W("tensor", lambda e, j=j, h=h: e.matmul(
                        ps2[:, h * 512:(h + 1) * 512], cst[:, 256:384],
                        tsb(slot)[:, j * 512:(j + 1) * 512],
                        start=True, stop=True, skip_group_check=True),
                        waits=mw, inc=("ten", 1))
                    ew = [("ten", mm)]
                    if j < 2 and tf_wps_dma[slot]:
                        ew.append((f"dwp{slot}", tf_wps_dma[slot]))
                    ev = W("vector", lambda e, j=j, h=h: e.tensor_scalar(
                        wps(slot)[:, j * 512:(j + 1) * 512],
                        ps2[:, h * 512:(h + 1) * 512], fcb[:, 2:3], None,
                        op0=ALU.mult),
                        waits=ew, inc=("vec", 1))
                    ps2_guard[h].append(("vec", ev))
                    tf_ev[(kt_, j)] = ev
                if j0 == 2:
                    tf_mm_done_k[kt_] = at("ten")
                    tf_wps_dma[slot] = W("sync", lambda e: e.dma_start(
                        out=w2p_dram[kt_ * 128:(kt_ + 1) * 128, :], in_=wps(slot)),
                        waits=[("vec", tf_ev[(kt_, j)]) for j in range(4)],
                        inc=(f"dwp{slot}", 16))
                    tf_wp_count[kt_] = tf_wps_dma[slot]

            return [(0, c_load), (2, lambda: c_mm(0)), (2, lambda: c_mm(2))]

        z2b = lambda slot: A4[:, 8192 + slot * 512: 8192 + (slot + 1) * 512]
        w2rb = lambda slot: A4[:, 4096 + slot * 512: 4096 + (slot + 1) * 512]
        BD = 8
        z2_done = [0] * BD
        w2r_done = [0] * BD

        def b_preload_chunks():
            def c_pre():
                for kp in range(BD):
                    zw = [("ten", g_done[15]),
                          ("dz0", zdma_done[2 * kp]), ("dz1", zdma_done[2 * kp + 1])]
                    z2_done[kp] = W("sync", lambda e, kp=kp: e.dma_start(
                        out=z2b(kp),
                        in_=z_dram[2 * kp:2 * kp + 2, :, 0:512]
                        .rearrange("s a t -> (s a) t")),
                        waits=zw, inc=(f"dz2{kp}", 16))
                    ww = [("ten", g_done[15]), (f"dwp{kp % 2}", tf_wp_count[kp])]
                    w2r_done[kp] = W("scalar", lambda e, kp=kp: e.dma_start(
                        out=w2rb(kp),
                        in_=w2p_dram[kp * 128:(kp + 1) * 128, 0:512]),
                        waits=ww, inc=(f"dw2{kp}", 16))
            return [(0, c_pre)]

        emit_w13_load(0)
        emit_w13_load(1)
        for u in range(9):
            for _c, _f in p3_chunks(0, u):
                _f()
        total_pairs = len(pair_list)
        sig_next = 0
        p3_units = deque()
        cur = deque()
        p3_pending = [0]

        def p2_rem():
            return (sum(c for c, _f, p3f in cur if not p3f)
                    + (128 - sig_next) * 33)

        def p3_rem():
            return (sum(sum(c for c, _f in u) for u in p3_units)
                    + sum(c for c, _f, p3f in cur if p3f))

        def pull(nmm):
            nonlocal sig_next
            got = 0
            while got < nmm:
                if not cur:
                    if p3_units:
                        cur.extend((c, f, True) for c, f in p3_units.popleft())
                        p3_pending[0] += len(cur)
                    elif sig_next < 128:
                        cur.extend((c, f, False) for c, f in p2_chunks(sig_next))
                        sig_next += 1
                    else:
                        break
                c, f, p3f = cur.popleft()
                if p3f:
                    p3_pending[0] -= 1
                f()
                got += c
            return got

        pairs_done = 0
        acc = 0.0
        for j in range(16):
            if j < 15:
                for u in range(9):
                    p3_units.append(p3_chunks(j + 1, u))
            else:
                for kt_i in range(64):
                    p3_units.append(tf_chunks(kt_i))
                    if kt_i == 7 and USE_B_PRELOAD:
                        p3_units.append(b_preload_chunks())
                    if kt_i % 4 == 3 and sig_next < 128:
                        p3_units.append(p2_chunks(sig_next))
                        sig_next += 1
            for pi in pairs_by_j[j]:
                pairs_rem = total_pairs - pairs_done
                quota = (p2_rem() + p3_rem()) / float(pairs_rem)
                acc += quota
                budget = [int(acc)]
                acc -= int(acc)

                def bpull(n, budget=budget):
                    got = pull(min(n, budget[0]))
                    budget[0] -= got
                emit_pair(pi, bpull)
                pull(budget[0])
                pairs_done += 1
            while p3_units or p3_pending[0]:
                pull(1)
        pull(10 ** 9)
        emit_av(total_pairs - 1)
        emit_epilogue(total_pairs - 1)
        z_unit(NSTEP - 2)
        z_unit(NSTEP - 1)
        z_all = (("dz0", at("dz0")), ("dz1", at("dz1")))

        if debug:
            W("sync", lambda e: e.dma_start(
                out=dbg["dbg_z"].rearrange("p (a t) -> p a t", a=64),
                in_=z_dram[:, :, :]),
                waits=list(z_all), inc=("dmisc", 16))
            W("sync", lambda e: e.dma_start(out=dbg["dbg_kt"], in_=kt[:, :]),
              waits=[("vec", at("vec"))], inc=("dmisc", 16))
            W("sync", lambda e: e.dma_start(out=dbg["dbg_vt"], in_=vt[:, :]), inc=("dmisc", 16))
            W("sync", lambda e: e.dma_start(out=dbg["dbg_g"], in_=g_dram[:, :]),
              waits=[("dgd", at("dgd"))], inc=("dmisc", 16))

        p4_vec_end = at("vec")
        p4_sca_end = at("sca")
        p3_ten_end = at("ten")

        def a2a_fn(e):
            with allow_ici_all_to_all():
                return e.collective_compute(
                    "AllToAll", ALU.bypass, replica_groups=[list(range(NCORE))],
                    ins=[a2a_in.ap().opt()], outs=[a2a_out.ap().opt()])
        W("gpsimd", a2a_fn,
          waits=[("dog0", last_ogdma[0]), ("dog1", last_ogdma[1])], inc=("cc2", 1))
        if debug:
            W("sync", lambda e: e.dma_start(out=dbg["dbg_a2a"], in_=a2a_out[:, :]),
              waits=[("cc2", 1)], inc=("dmisc", 16))

        wo3 = lambda dslot: A3[:, dslot * 8192:(dslot + 1) * 8192]
        wo_cnt = {}
        last_av_ten = av_read[total_pairs - 1]
        for ds0 in (0, 1):
            wo_cnt[ds0] = W("gpsimd", lambda e, ds0=ds0: e.dma_start(
                out=wo3(ds0).rearrange("p (hc d) -> p hc d", hc=16),
                in_=wout[:, ds0 * 512:(ds0 + 1) * 512]
                .rearrange("(hc p) d -> p hc d", p=128)),
                waits=[("ten", last_av_ten)], inc=(f"dwo{ds0}", 16))

        afT = lambda slot, i: AFl[:, i * 512:(i + 1) * 512]

        if debug:
            W("sync", lambda e: e.dma_start(out=dbg["dbg_w2p"], in_=w2p_dram[:, :]),
              waits=[("dwp0", at("dwp0")), ("dwp1", at("dwp1"))], inc=("dmisc", 16))

        b_psum = [ps2[:, 0:512], ps2[:, 512:1024], pbA[:, :], pbB[:, :]]
        bmm_done = {}
        bev_done = {}
        afslot_adds = [0, 0]
        last_add_AT = [[None] * 4, [None] * 4]
        p6_units = [(ds, tau) for ds in range(4) for tau in range(2)]
        first_bmm = True
        for ui, (ds, tau) in enumerate(p6_units):
            for kp in range(64):
                slot = kp % BD
                if not (USE_B_PRELOAD and ui == 0 and kp < BD):
                    zw = []
                    prev = bmm_done.get((ui, kp - BD))
                    if prev is None and ui > 0 and kp < BD:
                        prev = bmm_done.get((ui - 1, 64 - BD + kp))
                    if prev:
                        zw.append(("ten", prev))
                    if ui == 0:
                        if kp < BD:
                            zw.append(("ten", g_done[15]))
                        zw.append(("dz0", zdma_done[2 * kp]))
                        zw.append(("dz1", zdma_done[2 * kp + 1]))
                    z2_done[slot] = W("sync", lambda e, kp=kp, tau=tau, slot=slot: e.dma_start(
                        out=z2b(slot),
                        in_=z_dram[2 * kp:2 * kp + 2, :, tau * 512:(tau + 1) * 512]
                        .rearrange("s a t -> (s a) t")),
                        waits=zw, inc=(f"dz2{slot}", 16))
                    ww = [(f"dwp{kp % 2}", tf_wp_count[kp])] if ui == 0 else []
                    if ui == 0 and kp < BD:
                        ww.append(("ten", g_done[15]))
                    if prev:
                        ww.append(("ten", prev))
                    w2r_done[slot] = W("scalar", lambda e, kp=kp, ds=ds, slot=slot: e.dma_start(
                        out=w2rb(slot),
                        in_=w2p_dram[kp * 128:(kp + 1) * 128, ds * 512:(ds + 1) * 512]),
                        waits=ww, inc=(f"dw2{slot}", 16))
                mw = [(f"dz2{slot}", z2_done[slot]), (f"dw2{slot}", w2r_done[slot])]
                if kp == 0:
                    if first_bmm:
                        mw.append(("vec", at("vec")))
                        mw.append(("sca", at("sca")))
                        first_bmm = False
                    if ui >= 1:
                        mw.append(("vec", max(bev_done[(ui - 1, i)] for i in range(4))))
                for T in range(4):
                    r_ = W("tensor", lambda e, T=T, slot=slot, kp=kp: e.matmul(
                        b_psum[T], z2b(slot)[:, T * 128:(T + 1) * 128], w2rb(slot),
                        start=(kp == 0), stop=(kp == 63), skip_group_check=True),
                        waits=mw if T == 0 else [], inc=("ten", 1) if T == 3 else None)
                    if T == 3:
                        bmm_done[(ui, kp)] = r_
            afs = ui % 2
            for T in range(4):
                ew = [("ten", bmm_done[(ui, 63)])]
                if last_add_AT[afs][T] is not None:
                    ew.append(last_add_AT[afs][T])
                bev_done[(ui, T)] = W("vector", lambda e, T=T, afs=afs: e.tensor_copy(
                    afT(afs, T), b_psum[T]),
                    waits=ew, inc=("vec", 1))
            for T in range(4):
                aw = [("vec", bev_done[(ui, T)])]
                if ui == 0 and T == 0:
                    aw.append(("dxcp", xcp_done))
                afslot_adds[afs] = W("gpsimd", lambda e, T=T, tau=tau, ds=ds, afs=afs: e.dma_start(
                    out=out[tau * 512 + T * 128: tau * 512 + T * 128 + 128,
                            ds * 512:(ds + 1) * 512],
                    in_=afT(afs, T), accum_op=ALU.add),
                    waits=aw, inc=(f"dac{afs}", 16))
                last_add_AT[afs][T] = (f"dac{afs}", afslot_adds[afs])
        p6_adds = (("dac0", at("dac0")), ("dac1", at("dac1")))

        PD = 8
        og4 = lambda slot: A4[:, slot * 512:(slot + 1) * 512]
        og_done = [0] * PD
        pmm_done = {}
        pev_done = {}
        p5_units = [(ds, tcg) for ds in range(4) for tcg in range(2)]
        first_p5 = True
        for uix, (ds, tcg) in enumerate(p5_units):
            dslot = ds % 2
            if tcg == 0 and ds >= 2:
                wo_cnt[ds] = W("gpsimd", lambda e, ds=ds, dslot=dslot: e.dma_start(
                    out=wo3(dslot).rearrange("p (hc d) -> p hc d", hc=16),
                    in_=wout[:, ds * 512:(ds + 1) * 512]
                    .rearrange("(hc p) d -> p hc d", p=128)),
                    waits=[("ten", pmm_done[((ds - 2) * 2 + 1, 15)])],
                    inc=(f"dwo{dslot}", 16))
            for hc in range(16):
                slot = hc % PD
                prev = pmm_done.get((uix, hc - PD))
                if prev is None and uix > 0 and hc < PD:
                    prev = pmm_done.get((uix - 1, 16 - PD + hc))
                ogw = [("cc2", 1)] if first_p5 else []
                if prev:
                    ogw.append(("ten", prev))
                og_done[slot] = W("sync", lambda e, hc=hc, tcg=tcg, slot=slot: e.dma_start(
                    out=og4(slot),
                    in_=a2a_out[(hc % 8) * 256 + (hc // 8) * 128:
                                (hc % 8) * 256 + (hc // 8) * 128 + 128,
                                tcg * 512:(tcg + 1) * 512]),
                    waits=ogw, inc=(f"dog4{slot}", 16))
                first_p5 = False
                mw = [(f"dog4{slot}", og_done[slot])]
                if hc == 0:
                    mw.append((f"dwo{dslot}", wo_cnt[ds]))
                    if uix == 0:
                        mw.append(("vec", max(bev_done[(7, i)] for i in range(4))))
                    else:
                        mw.append(("vec", max(pev_done[(uix - 1, i)] for i in range(4))))
                for T in range(4):
                    r_ = W("tensor", lambda e, T=T, slot=slot, hc=hc, dslot=dslot: e.matmul(
                        b_psum[T], og4(slot)[:, T * 128:(T + 1) * 128],
                        wo3(dslot)[:, hc * 512:(hc + 1) * 512],
                        start=(hc == 0), stop=(hc == 15), skip_group_check=True),
                        waits=mw if T == 0 else [], inc=("ten", 1) if T == 3 else None)
                    if T == 3:
                        pmm_done[(uix, hc)] = r_
            afs = uix % 2
            for T in range(4):
                ew = [("ten", pmm_done[(uix, 15)])]
                if last_add_AT[afs][T] is not None:
                    ew.append(last_add_AT[afs][T])
                if uix < 2:
                    ew.extend(p6_adds)
                pev_done[(uix, T)] = W("vector", lambda e, T=T, afs=afs: e.tensor_copy(
                    afT(afs, T), b_psum[T]),
                    waits=ew, inc=("vec", 1))
            for T in range(4):
                aw = [("vec", pev_done[(uix, T)])]
                afslot_adds[afs] = W("gpsimd", lambda e, T=T, tcg=tcg, ds=ds, afs=afs: e.dma_start(
                    out=out[tcg * 512 + T * 128: tcg * 512 + T * 128 + 128,
                            ds * 512:(ds + 1) * 512],
                    in_=afT(afs, T), accum_op=ALU.add),
                    waits=aw, inc=(f"dac{afs}", 16))
                last_add_AT[afs][T] = (f"dac{afs}", afslot_adds[afs])

        W("gpsimd", lambda e: e.nop(),
          waits=[("dac0", at("dac0")), ("dac1", at("dac1")), ("dxcp", xcp_done)])
        W("sync", lambda e: e.nop(),
          waits=[("dac0", at("dac0")), ("dac1", at("dac1"))])

        with nc.Block() as block:
            def mk(ename):
                def emit(eng):
                    lastw = {}
                    for fn, waits, inc in sch.q[ename]:
                        for s_, v in waits:
                            if v > 0 and lastw.get(s_, -1) < v:
                                eng.wait_ge(sems[s_], v)
                                lastw[s_] = v
                        ins = fn(eng)
                        if inc is not None:
                            ins.then_inc(sems[inc[0]], inc[1])
                return emit
            block.tensor(mk("tensor"))
            block.scalar(mk("scalar"))
            block.vector(mk("vector"))
            block.gpsimd(mk("gpsimd"))
            block.sync(mk("sync"))

    return nc


def make_inputs(x, w_qkv, w_gates, w_out, W1, W2, W3):
    x = np.ascontiguousarray(np.asarray(x, np.float32).reshape(S, D))
    w_qkv = np.asarray(w_qkv, np.float32)
    w_gates = np.asarray(w_gates, np.float32)
    w_out = np.ascontiguousarray(np.asarray(w_out, np.float32))
    w1t = np.ascontiguousarray(np.asarray(W1, np.float32).T)
    w3t = np.ascontiguousarray(np.asarray(W3, np.float32).T)
    w2t = np.asarray(W2, np.float32).T
    w2tp = np.ascontiguousarray(w2t.reshape(64, 128, D).transpose(1, 0, 2).reshape(FF, D))
    cst = build_consts()
    fcv = np.zeros((128, 4), np.float32)
    fcv[:, 0] = EPS
    fcv[:, 1] = -1.0
    fcv[:, 2] = FWHT_SCALE * 0.5
    fcv[:, 3] = 1.0
    in_maps = []
    for c in range(NCORE):
        cols = []
        for kind in range(3):
            for s_ in range(2):
                h = c + 8 * s_
                cols.append(np.arange(kind * 2048 + h * 128, kind * 2048 + h * 128 + 128))
        cols = np.concatenate(cols)
        in_maps.append({
            "x": np.ascontiguousarray(x[c * TOWN:(c + 1) * TOWN]),
            "wqkv": np.ascontiguousarray(w_qkv[:, cols]),
            "wg": np.ascontiguousarray(w_gates[:, [c, c + 8]]),
            "wout": w_out,
            "w1t": w1t,
            "w3t": w3t,
            "w2tp": w2tp,
            "cst": cst,
            "fc": fcv,
        })
    return in_maps


def kernel(L0_attn_x, L0_attn_w_qkv, L0_attn_w_gates, L0_attn_w_out,
           L0_ffn_W1, L0_ffn_W2, L0_ffn_W3):
    in_maps = make_inputs(L0_attn_x, L0_attn_w_qkv, L0_attn_w_gates,
                          L0_attn_w_out, L0_ffn_W1, L0_ffn_W2, L0_ffn_W3)
    nc = build_nc()
    res = run_bass_kernel_spmd(nc, in_maps, core_ids=list(range(NCORE)))
    out = np.concatenate([np.asarray(res.results[c]["out"], np.float32)
                          for c in range(NCORE)], axis=0)
    return out.reshape(1, S, D)


if __name__ == "__main__":
    import time
    t0 = time.time()
    build_nc(debug=False)
    print(f"build ok in {time.time() - t0:.1f}s")


# version 24
# speedup vs baseline: 1.0026x; 1.0026x over previous
from collections import deque
from contextlib import ExitStack

import numpy as np
import ml_dtypes

import concourse.bass as bass
import concourse.mybir as mybir
from concourse.bass_utils import run_bass_kernel_spmd
from concourse.aot_env import allow_ici_all_to_all

F32 = mybir.dt.float32
BF16 = mybir.dt.bfloat16
AF = mybir.ActivationFunctionType
ALU = mybir.AluOpType

NCORE = 8
S = 8192
D = 2048
DH = 128
FF = 8192
TOWN = 1024
NTC = 16
TANH_SCALE = float((128.0 ** -0.5) / 50.0)
EXP_SCALE = 50.0
FWHT_SCALE = float(2.0 ** -6.5)
EPS = 1e-5

DEBUG = False
USE_B_PRELOAD = False

ENGS = ("tensor", "scalar", "vector", "gpsimd", "sync")


class Sched:
    def __init__(self):
        self.q = {e: [] for e in ENGS}
        self.cnt = {}

    def op(self, eng, fn, waits=(), inc=None):
        if inc is not None:
            s, amt = inc
            self.cnt[s] = self.cnt.get(s, 0) + amt
        self.q[eng].append((fn, tuple(waits), inc))
        return self.cnt[inc[0]] if inc is not None else None

    def at(self, s):
        return self.cnt.get(s, 0)


def _hadamard(n):
    h = np.array([[1.0]], np.float32)
    while h.shape[0] < n:
        h = np.block([[h, h], [h, -h]])
    return h


def build_consts():
    c = np.zeros((128, 2688), np.float32)
    c[:, 0:128] = np.eye(128)
    c[:, 128:256] = _hadamard(128)
    c[:, 256:384] = np.kron(np.eye(2), _hadamard(64))
    c[:, 384:512] = 1.0
    ii = np.arange(512)[None, :]
    pp = np.arange(128)[:, None]
    for jj in range(4):
        c[:, 512 + jj * 512: 1024 + jj * 512] = (jj * 128 + pp <= ii).astype(np.float32)
    c[:, 2560:2561] = 1.0
    c[:, 2561:2562] = 2.0
    return c.astype(ml_dtypes.bfloat16)


def build_nc(debug=DEBUG):
    nc = bass.Bass("TRN2", target_bir_lowering=False, debug=False)

    x_in = nc.dram_tensor("x", [TOWN, D], F32, kind="ExternalInput").ap()
    wqkv = nc.dram_tensor("wqkv", [D, 768], F32, kind="ExternalInput").ap()
    wg = nc.dram_tensor("wg", [D, 2], F32, kind="ExternalInput").ap()
    wout = nc.dram_tensor("wout", [D, D], F32, kind="ExternalInput").ap()
    w1t = nc.dram_tensor("w1t", [D, FF], F32, kind="ExternalInput").ap()
    w3t = nc.dram_tensor("w3t", [D, FF], F32, kind="ExternalInput").ap()
    w2tp = nc.dram_tensor("w2tp", [FF, D], F32, kind="ExternalInput").ap()
    cst_in = nc.dram_tensor("cst", [128, 2688], BF16, kind="ExternalInput").ap()
    fc_in = nc.dram_tensor("fc", [128, 4], F32, kind="ExternalInput").ap()
    out = nc.dram_tensor("out", [TOWN, D], F32, kind="ExternalOutput").ap()

    ag1_in = nc.dram_tensor("ag1_in", [D, TOWN], BF16)
    ag1_out = nc.dram_tensor("ag1_out", [NCORE * D, TOWN], BF16, addr_space="Shared")
    a2a_in = nc.dram_tensor("a2a_in", [2048, TOWN], BF16)
    a2a_out = nc.dram_tensor("a2a_out", [2048, TOWN], BF16)
    z_dram = nc.dram_tensor("z_dram", [128, 64, TOWN], BF16)
    w2p_dram = nc.dram_tensor("w2p_dram", [FF, D], BF16)
    g_dram = nc.dram_tensor("g_dram", [2, S], F32)

    dbg = {}
    if debug:
        for name, shape, dt in [
            ("dbg_xnt", [128, 16 * TOWN], BF16),
            ("dbg_kt", [128, 2 * S], BF16),
            ("dbg_vt", [128, 2 * S], BF16),
            ("dbg_g", [2, S], F32),
            ("dbg_z", [128, 64 * TOWN], BF16),
            ("dbg_w2p", [FF, D], BF16),
            ("dbg_a2a", [2048, TOWN], BF16),
        ]:
            dbg[name] = nc.dram_tensor(name, shape, dt, kind="ExternalOutput").ap()

    sch = Sched()

    with ExitStack() as st:
        sb = lambda n, s_, d: st.enter_context(nc.sbuf_tensor(n, s_, d))
        psT = lambda n, s_, d: st.enter_context(nc.psum_tensor(n, s_, d))

        cst = sb("cst_sb", [128, 2688], BF16)
        fcb = sb("fcb", [128, 4], F32)
        A1 = sb("A1", [128, 16384], BF16)
        A2 = sb("A2", [128, 12288], BF16)
        A3 = sb("A3", [128, 16384], BF16)
        A4 = sb("A4", [128, 12288], BF16)
        A5 = sb("A5", [128, 12800], BF16)
        KT = sb("KT", [128, 16384], BF16)
        qsm = sb("qsm", [128, 2048], BF16)
        AFl = sb("AF", [128, 2048], F32)
        vtmp = sb("vtmp", [128, 1024], BF16)
        wgb = sb("wgb", [128, 32], BF16)
        bn24 = sb("bn24", [128, 24], F32)
        mv = sb("mv", [128, 2], F32)
        nm = sb("nm", [128, 1], F32)
        sq = sb("sq", [128, 1], F32)
        rs = sb("rs", [128, 1], F32)
        g2sb = sb("g2sb", [2, 512], F32)
        g_sb = sb("g_sb", [1, 512], F32)
        rd_sb = sb("rd_sb", [1, 512], F32)
        scs = sb("scs", [1, 512], BF16)
        bcs = sb("bcs", [128, 512], BF16)
        dacc = sb("dacc", [128, 512], F32)
        dbc = sb("dbc", [128, 512], BF16)
        gt_sb = sb("gt_sb", [1, 512], F32)

        ps_s = psT("ps_s", [128, 1024], F32)
        ps2 = psT("ps2", [128, 1024], F32)
        pbA = psT("pbA", [128, 512], F32)
        pbB = psT("pbB", [128, 512], F32)
        pTa = psT("pTa", [128, 1024], BF16)
        pTb = psT("pTb", [128, 1024], BF16)

        sem_names = (
            ["ten", "sca", "vec", "cc1", "cc2", "dmisc", "dfc", "dgw", "dagi",
             "dxf0", "dxf1", "dw0", "dw1", "dz0", "dz1",
             "dxa0", "dxa1", "dgd", "dg4", "dog0", "dog1",
             "dwt0", "dwt1", "dwp0", "dwp1", "dac0", "dac1", "dxcp"]
            + [f"dz2{i}" for i in range(8)] + [f"dw2{i}" for i in range(8)]
            + [f"dwo{i}" for i in range(8)] + [f"dog4{i}" for i in range(8)]
        )
        sems = {n: st.enter_context(nc.semaphore(n)) for n in sem_names}

        W = sch.op
        at = sch.at

        pset = lambda i: ps_s if i % 2 == 0 else ps2

        xcp_done = W("sync", lambda e: e.dma_start(out=out, in_=x_in), inc=("dxcp", 16))
        cst_done = W("sync", lambda e: e.dma_start(out=cst[:, :], in_=cst_in), inc=("dmisc", 16))
        fc_done = W("sync", lambda e: e.dma_start(out=fcb[:, :], in_=fc_in), inc=("dfc", 16))
        wqkvb = A4
        W("gpsimd", lambda e: e.dma_start(
            out=wqkvb[:, 0:12288].rearrange("p (dc m) -> p dc m", dc=16),
            in_=wqkv.rearrange("(dc p) m -> p dc m", p=128)), inc=("dgw", 16))
        wg_done = W("gpsimd", lambda e: e.dma_start(
            out=wgb[:, :].rearrange("p (dc g) -> p dc g", dc=16),
            in_=wg.rearrange("(dc p) g -> p dc g", p=128)), inc=("dgw", 16))

        xnb = KT
        xnT = A1
        apply_done = [0] * 8
        prev_sqv = 0
        prev_bnv = 0
        for tt in range(8):
            slot = tt % 2
            waits = []
            if tt >= 1:
                waits.append(("vec", apply_done[tt - 1]))
            dxf = W("sync", lambda e, tt=tt: e.dma_start(
                out=AFl[:, 0:2048],
                in_=x_in[tt * 128:(tt + 1) * 128, :]),
                waits=waits, inc=(f"dxf{slot}", 16))
            xfv = lambda: AFl[:, 0:2048]
            for i in range(4):
                w_ = []
                if i == 0:
                    w_.append((f"dxf{slot}", dxf))
                    if tt > 0:
                        w_.append(("vec", prev_bnv))
                W("vector", lambda e, i=i: e.bn_stats(
                    bn24[:, 6 * i:6 * i + 6], xfv()[:, 512 * i:512 * i + 512]),
                    waits=w_, inc=("vec", 1))
            bw = [("vec", at("vec"))]
            if tt > 0:
                bw.append(("sca", prev_sqv))
            bnv = W("vector", lambda e: e.bn_aggr(mv[:, :], bn24[:, :]),
                    waits=bw, inc=("vec", 1))
            prev_bnv = bnv
            nmv = W("vector", lambda e: e.tensor_scalar(
                nm[:, :], mv[:, 0:1], fcb[:, 1:2], None, op0=ALU.mult),
                waits=[("vec", bnv), ("dfc", fc_done)], inc=("vec", 1))
            sqv = W("scalar", lambda e: e.activation(sq[:, :], mv[:, 1:2], AF.Sqrt,
                                                     bias=fcb[:, 0:1]),
                    waits=[("vec", bnv), ("dfc", fc_done)], inc=("sca", 1))
            prev_sqv = sqv
            rsv = W("vector", lambda e: e.reciprocal(rs[:, :], sq[:, :]),
                    waits=[("sca", sqv), ("vec", nmv)], inc=("vec", 1))
            apply_done[tt] = W("vector", lambda e, tt=tt: e.tensor_scalar(
                xnb[:, tt * 2048:(tt + 1) * 2048], xfv()[:, :], nm[:, 0:1], rs[:, 0:1],
                op0=ALU.add, op1=ALU.mult),
                waits=[("vec", rsv)], inc=("vec", 1))

        ln_all = apply_done[7]
        tp_done = [0, 0]
        ev_done = [0, 0]
        for dc in range(16):
            slot = dc % 2
            pT = pTa if slot == 0 else pTb
            for tt in range(8):
                w_ = []
                if dc == 0 and tt == 0:
                    w_ = [("vec", ln_all), ("dmisc", cst_done)]
                if tt == 0 and dc >= 2:
                    w_.append(("vec", ev_done[slot]))
                r_ = W("tensor", lambda e, dc=dc, tt=tt, pT=pT: e.transpose(
                    pT[:, tt * 128:(tt + 1) * 128],
                    xnb[:, tt * 2048 + dc * 128: tt * 2048 + dc * 128 + 128],
                    cst[:, 0:128]),
                    waits=w_, inc=("ten", 1) if tt == 7 else None)
                if tt == 7:
                    tp_done[slot] = r_
            ev_done[slot] = W("vector", lambda e, dc=dc, pT=pT: e.tensor_copy(
                xnT[:, dc * 1024:(dc + 1) * 1024], pT[:, :]),
                waits=[("ten", tp_done[slot])], inc=("vec", 1))
        xnt_ready_all = max(ev_done)

        agi = 0
        for dc in range(16):
            agi = W("sync", lambda e, dc=dc: e.dma_start(
                out=ag1_in[dc * 128:(dc + 1) * 128, :],
                in_=xnT[:, dc * 1024:(dc + 1) * 1024]),
                waits=[("vec", ev_done[dc % 2])], inc=("dagi", 16))
        W("gpsimd", lambda e: e.collective_compute(
            "AllGather", ALU.bypass, replica_groups=[list(range(NCORE))],
            ins=[ag1_in.ap().opt()], outs=[ag1_out.ap().opt()]),
            waits=[("dagi", agi)], inc=("cc1", 1))

        if debug:
            W("sync", lambda e: e.dma_start(out=dbg["dbg_xnt"], in_=xnT[:, :]),
              waits=[("vec", xnt_ready_all)], inc=("dmisc", 16))

        w13 = lambda slot: A2[:, slot * 4096:(slot + 1) * 4096]
        hbuf = lambda slot: A2[:, 8192 + slot * 512: 8192 + (slot + 1) * 512]
        stmp = lambda slot: A2[:, 9216 + slot * 512: 9216 + (slot + 1) * 512]
        zstage = lambda slot: A2[:, 10240 + slot * 512: 10240 + (slot + 1) * 512]
        s1b = lambda slot: A2[:, 11264 + slot * 512: 11264 + (slot + 1) * 512]

        kt, vt = KT, A3
        xadst = lambda slot: A5[:, 0:8192]
        tts = A5[:, 8192:9216]
        pts = lambda slot: A5[:, 9216 + slot * 1024: 9216 + (slot + 1) * 1024]
        og_stage = lambda slot: A5[:, 11264 + slot * 512: 11264 + (slot + 1) * 512]

        ps2_guard = [[], []]
        pbB_guard = []

        def guard_waits(g):
            w = list(g)
            del g[:]
            return w

        NSTEP = 128
        upmm_done = [0] * NSTEP
        tsil_done = [0] * NSTEP
        s1_done = [0] * NSTEP
        hm_done = [0] * NSTEP
        zmm_done = [0] * NSTEP
        zev_done = [0] * NSTEP
        zdma_done = [0] * NSTEP
        w13_done = [0] * 64

        def z_unit(sig):
            if sig < 0 or sig >= NSTEP:
                return
            chi, tau = divmod(sig, 2)
            hslot = sig % 2
            waits = [("vec", hm_done[sig])] + guard_waits(pbB_guard)
            zmm_done[sig] = W("tensor", lambda e, hslot=hslot: e.matmul(
                pbB[:, :], cst[:, 128:256], hbuf(hslot),
                start=True, stop=True, skip_group_check=True),
                waits=waits, inc=("ten", 1))
            zw = [("ten", zmm_done[sig])]
            if sig >= 2:
                zw.append((f"dz{hslot}", zdma_done[sig - 2]))
            zev_done[sig] = W("vector", lambda e, hslot=hslot: e.tensor_copy(
                zstage(hslot), pbB[:, :]),
                waits=zw, inc=("vec", 1))
            pbB_guard.append(("vec", zev_done[sig]))
            zdma_done[sig] = W("sync", lambda e, chi=chi, tau=tau, hslot=hslot: e.dma_start(
                out=z_dram[:, chi, tau * 512:(tau + 1) * 512], in_=zstage(hslot)),
                waits=[("vec", zev_done[sig])], inc=(f"dz{hslot}", 16))

        def emit_w13_load(chi):
            wslot = chi % 2
            w_ = []
            if chi >= 2:
                w_.append(("ten", upmm_done[2 * (chi - 2) + 1]))
            for wi, wt in ((0, w1t), (1, w3t)):
                w13_done[chi] = W("gpsimd", lambda e, chi=chi, wi=wi, wt=wt, wslot=wslot: e.dma_start(
                    out=w13(wslot)[:, wi * 2048:(wi + 1) * 2048].rearrange("p (dc f) -> p dc f", dc=16),
                    in_=wt[:, chi * 128:(chi + 1) * 128].rearrange("(dc p) f -> p dc f", p=128)),
                    waits=w_ if wi == 0 else [], inc=(f"dw{wslot}", 16))

        def p2_chunks(sig):
            chi, tau = divmod(sig, 2)
            wslot = chi % 2
            hslot = sig % 2

            def up_half(wi, half):
                ww = []
                if half == 0:
                    ww = guard_waits(ps2_guard[wi])
                    if wi == 0:
                        ww.append((f"dw{wslot}", w13_done[chi]))
                        if sig == 0:
                            ww.append(("vec", xnt_ready_all))
                for dc in range(half * 8, half * 8 + 8):
                    W("tensor", lambda e, wi=wi, dc=dc: e.matmul(
                        ps2[:, wi * 512:(wi + 1) * 512],
                        w13(wslot)[:, wi * 2048 + dc * 128: wi * 2048 + dc * 128 + 128],
                        xnT[:, dc * 1024 + tau * 512: dc * 1024 + tau * 512 + 512],
                        start=(dc == 0), stop=(dc == 15), skip_group_check=True),
                        waits=ww if dc == half * 8 else [],
                        inc=("ten", 1) if dc == 15 else None)

            def c_wi0b():
                up_half(0, 1)
                upg = at("ten")
                tsw = [("ten", upg)]
                if sig >= 2:
                    tsw.append(("vec", s1_done[sig - 2]))
                tsil_done[sig] = W("scalar", lambda e: e.activation(
                    stmp(hslot), ps2[:, 0:512], AF.Tanh, scale=0.5),
                    waits=tsw, inc=("sca", 1))
                s1w = [("sca", tsil_done[sig]), ("dfc", fc_done)]
                if sig >= 2:
                    s1w.append(("ten", zmm_done[sig - 2]))
                s1_done[sig] = W("vector", lambda e: e.scalar_tensor_tensor(
                    s1b(hslot), stmp(hslot), fcb[:, 3:4], ps2[:, 0:512],
                    op0=ALU.add, op1=ALU.mult),
                    waits=s1w, inc=("vec", 1))
                ps2_guard[0].extend([("sca", tsil_done[sig]), ("vec", s1_done[sig])])

            def c_wi1b():
                up_half(1, 1)
                upmm_done[sig] = at("ten")
                hw = [("vec", s1_done[sig]), ("ten", upmm_done[sig])]
                if sig >= 2:
                    hw.append(("ten", zmm_done[sig - 2]))
                hm_done[sig] = W("vector", lambda e: e.tensor_tensor(
                    hbuf(hslot), s1b(hslot), ps2[:, 512:1024], op=ALU.mult),
                    waits=hw, inc=("vec", 1))
                ps2_guard[1].append(("vec", hm_done[sig]))
                if tau == 1 and chi + 2 < 64:
                    emit_w13_load(chi + 2)

            return [(1, lambda: z_unit(sig - 2)),
                    (8, lambda: up_half(0, 0)), (8, c_wi0b),
                    (8, lambda: up_half(1, 0)), (8, c_wi1b)]

        m_evac_done = [[0] * 6 for _ in range(NTC)]
        g_done = [0] * NTC
        gev_done = [0] * NTC
        vtr_done = [0] * NTC
        vte_done = [0] * NTC
        xad_cnt = [0] * NTC
        gdma = 0

        def p3_chunks(tc, u):
            slot = tc % 2

            if u == 0:
                def c_dma():
                    r, wnd = divmod(tc, 2)
                    w_ = [("cc1", 1)]
                    if tc >= 1:
                        w_.append(("ten", g_done[tc - 1]))
                    xad_cnt[tc] = W("sync", lambda e, r=r, wnd=wnd: e.dma_start(
                        out=xadst(slot).rearrange("p (dc t) -> p dc t", dc=16),
                        in_=ag1_out[r * 2048:(r + 1) * 2048, wnd * 512:(wnd + 1) * 512]
                        .rearrange("(dc p) t -> p dc t", p=128)),
                        waits=w_, inc=(f"dxa{slot}", 16))
                return [(0, c_dma)]

            if 1 <= u <= 6:
                m = u - 1
                half = m % 2
                pdst = ps2[:, half * 512:(half + 1) * 512]

                def m_half(hh):
                    ww = []
                    if hh == 0:
                        ww = [(f"dxa{slot}", xad_cnt[tc])] + guard_waits(ps2_guard[half])
                        if tc == 0 and m == 0:
                            ww.append(("dgw", wg_done))
                    for dc in range(hh * 8, hh * 8 + 8):
                        W("tensor", lambda e, dc=dc: e.matmul(
                            pdst, wqkvb[:, dc * 768 + m * 128: dc * 768 + m * 128 + 128],
                            xadst(slot)[:, dc * 512:(dc + 1) * 512],
                            start=(dc == 0), stop=(dc == 15), skip_group_check=True),
                            waits=ww if dc == hh * 8 else [],
                            inc=("ten", 1) if dc == 15 else None)

                def c_mb():
                    m_half(1)
                    last = at("ten")
                    if m < 2:
                        dview = (lambda s_=m, tc=tc:
                                 qsm[:, (tc % 2) * 1024 + s_ * 512: (tc % 2) * 1024 + s_ * 512 + 512])
                    elif m < 4:
                        dview = (lambda s_=m - 2, tc=tc:
                                 kt[:, s_ * S + tc * 512: s_ * S + tc * 512 + 512])
                    else:
                        dview = (lambda s_=m - 4: vtmp[:, s_ * 512:(s_ + 1) * 512])
                    ew = [("ten", last)]
                    if tc == 0 and m == 0:
                        ew.append(("dagi", agi))
                    if m < 2 and tc >= 2:
                        ew.append(("ten", last_smm_of_j[tc - 2]))
                    if m >= 4 and tc >= 1:
                        ew.append(("ten", vtr_done[tc - 1]))
                    m_evac_done[tc][m] = W("vector", lambda e, dview=dview:
                                           e.tensor_copy(dview(), pdst),
                                           waits=ew, inc=("vec", 1))
                    ps2_guard[half].append(("vec", m_evac_done[tc][m]))
                return [(8, lambda: m_half(0)), (8, c_mb)]

            if u == 7:
                def g_half(hh):
                    ww = []
                    if hh == 0:
                        ww = [(f"dxa{slot}", xad_cnt[tc])] + guard_waits(ps2_guard[0])
                    for dc in range(hh * 8, hh * 8 + 8):
                        W("tensor", lambda e, dc=dc: e.matmul(
                            ps2[0:2, 0:512], wgb[:, dc * 2:(dc + 1) * 2],
                            xadst(slot)[:, dc * 512:(dc + 1) * 512],
                            start=(dc == 0), stop=(dc == 15), skip_group_check=True),
                            waits=ww if dc == hh * 8 else [],
                            inc=("ten", 1) if dc == 15 else None)

                def c_gb():
                    g_half(1)
                    g_done[tc] = at("ten")
                    gv = [("ten", g_done[tc])]
                    if tc >= 1:
                        gv.append(("dgd", sch.at("dgd")))
                    gev_done[tc] = W("vector", lambda e: e.tensor_copy(g2sb[:, :], ps2[0:2, 0:512]),
                                     waits=gv, inc=("vec", 1))
                    ps2_guard[0].append(("vec", gev_done[tc]))
                    for s_ in range(2):
                        W("gpsimd", lambda e, s_=s_: e.dma_start(
                            out=g_dram[s_:s_ + 1, tc * 512:(tc + 1) * 512], in_=g2sb[s_:s_ + 1, :]),
                            waits=[("vec", gev_done[tc])] if s_ == 0 else [], inc=("dgd", 16))
                return [(8, lambda: g_half(0)), (8, c_gb)]

            def c_vt():
                vw = [("vec", m_evac_done[tc][4]), ("vec", m_evac_done[tc][5])]
                if tc >= 1:
                    vw.append(("vec", vte_done[tc - 1]))
                for s_ in range(2):
                    for qq in range(4):
                        W("tensor", lambda e, s_=s_, qq=qq: e.transpose(
                            pTa[:, (s_ * 4 + qq) * 128:(s_ * 4 + qq) * 128 + 128],
                            vtmp[:, s_ * 512 + qq * 128: s_ * 512 + qq * 128 + 128],
                            cst[:, 0:128]),
                            waits=vw if (s_ == 0 and qq == 0) else [],
                            inc=("ten", 1) if (s_ == 1 and qq == 3) else None)
                vtr_done[tc] = at("ten")
                v1 = W("vector", lambda e, tc=tc: e.tensor_copy(
                    vt[:, tc * 512: tc * 512 + 512], pTa[:, 0:512]),
                    waits=[("ten", vtr_done[tc])], inc=("vec", 1))
                vte_done[tc] = W("vector", lambda e, tc=tc: e.tensor_copy(
                    vt[:, S + tc * 512: S + tc * 512 + 512], pTa[:, 512:1024]),
                    waits=[("vec", v1)], inc=("vec", 1))
            return [(3, c_vt)]

        pair_list = []
        pairs_by_j = {}
        gi = 0
        for qi in range(NTC):
            for s_ in range(2):
                npairs = 2 * (qi + 1)
                for p in range(npairs):
                    pairs_by_j.setdefault(qi, []).append(len(pair_list))
                    pair_list.append((gi, s_, qi, p, npairs))
                gi += 1

        g4_done = 0
        scs_done = 0
        last_smm_of_j = [0] * NTC
        tanh_done = {}
        exp_done = {}
        av_read = {}
        dacc_done = {}
        last_ogm = 0
        last_ogdma = [0, 0]
        last_rdv = 0
        last_bcev = 0
        last_bcmm = 0
        last_onesmm = 0
        last_dbc = 0
        last_gt = 0
        first_p4 = True

        def emit_av(gp):
            gi, s_, qi, p, npairs = pair_list[gp]
            pslot = gp % 2
            diag = (p >= npairs - 2)
            mw = [("vec", dacc_done[gp])] if diag else [("sca", exp_done[gp])]
            if p == 0 and last_ogm:
                mw.append(("vec", last_ogm))
            for half in range(2):
                jb = 2 * p + half
                W("tensor", lambda e, s_=s_, jb=jb, pslot=pslot, half=half, p=p,
                    npairs=npairs: e.matmul(
                    pbA[:, :], vt[:, s_ * S + jb * 128: s_ * S + jb * 128 + 128],
                    pts(pslot)[:, half * 512:(half + 1) * 512],
                    start=(p == 0 and half == 0), stop=(p == npairs - 1 and half == 1),
                    skip_group_check=True),
                    waits=mw if half == 0 else [], inc=("ten", 1) if half == 1 else None)
            av_read[gp] = at("ten")

        def emit_epilogue(gp, pull=None):
            nonlocal scs_done, last_rdv, last_bcev, last_bcmm, last_ogm
            nonlocal last_onesmm, last_dbc, last_gt
            gi, s_, qi, p, npairs = pair_list[gp]
            dw = [("vec", dacc_done[gp])]
            if last_onesmm:
                dw.append(("ten", last_onesmm))
            last_dbc = W("vector", lambda e: e.tensor_copy(dbc[:, :], dacc[:, :]),
                         waits=dw, inc=("vec", 1))
            if pull:
                pull(3)
            omw = [("vec", last_dbc)] + guard_waits(pbB_guard)
            last_onesmm = W("tensor", lambda e: e.matmul(
                pbB[0:1, :], cst[:, 2561:2562], dbc[:, :],
                start=True, stop=True, skip_group_check=True),
                waits=omw, inc=("ten", 1))
            gtw = [("dg4", g4_done)]
            last_gt = W("scalar", lambda e: e.activation(gt_sb[:, :], g_sb[:, :],
                                                         AF.Tanh, scale=0.5),
                        waits=gtw, inc=("sca", 1))
            rw = [("ten", last_onesmm)]
            last_rdv = W("vector", lambda e: e.reciprocal(rd_sb[:, :], pbB[0:1, :]),
                         waits=rw, inc=("vec", 1))
            scw = [("vec", last_rdv), ("sca", last_gt)]
            if last_bcmm:
                scw.append(("ten", last_bcmm))
            scs_done = W("vector", lambda e: e.scalar_tensor_tensor(
                scs[:, :], gt_sb[:, :], fcb[0:1, 3:4], rd_sb[:, :],
                op0=ALU.add, op1=ALU.mult),
                waits=scw, inc=("vec", 1))
            if pull:
                pull(3)
            last_bcmm = W("tensor", lambda e: e.matmul(
                pbB[:, :], cst[0:1, 384:512], scs[:, :],
                start=True, stop=True, skip_group_check=True),
                waits=[("vec", scs_done)], inc=("ten", 1))
            last_bcev = W("vector", lambda e: e.tensor_copy(bcs[:, :], pbB[:, :]),
                          waits=[("ten", last_bcmm)], inc=("vec", 1))
            pbB_guard.append(("vec", last_bcev))
            ogslot = gi % 2
            ogw = [("vec", last_bcev), ("ten", av_read[gp])]
            if last_ogdma[ogslot]:
                ogw.append((f"dog{ogslot}", last_ogdma[ogslot]))
            last_ogm = W("vector", lambda e, ogslot=ogslot: e.tensor_tensor(
                og_stage(ogslot), pbA[:, :], bcs[:, :], op=ALU.mult),
                waits=ogw, inc=("vec", 1))
            jdst, wdst = divmod(qi, 2)
            last_ogdma[ogslot] = W("sync", lambda e, jdst=jdst, wdst=wdst, s_=s_,
                                   ogslot=ogslot: e.dma_start(
                out=a2a_in[jdst * 256 + s_ * 128: jdst * 256 + s_ * 128 + 128,
                           wdst * 512:(wdst + 1) * 512],
                in_=og_stage(ogslot)),
                waits=[("vec", last_ogm)], inc=(f"dog{ogslot}", 16))

        def emit_pair(gp, pull=None):
            nonlocal first_p4, g4_done
            gi, s_, qi, p, npairs = pair_list[gp]
            pslot = gp % 2
            diag = (p >= npairs - 2)
            sw = []
            if first_p4:
                sw = [("dmisc", cst_done)]
                first_p4 = False
            if gp >= 1:
                sw.append(("sca", tanh_done[gp - 1]))
            if p == 0:
                sw.append(("vec", max(m_evac_done[qi])))
                sw.append(("vec", vte_done[qi]))
            for half in range(2):
                jb = 2 * p + half
                W("tensor", lambda e, s_=s_, qi=qi, jb=jb, half=half: e.matmul(
                    ps_s[:, half * 512:(half + 1) * 512],
                    kt[:, s_ * S + jb * 128: s_ * S + jb * 128 + 128],
                    qsm[:, (qi % 2) * 1024 + s_ * 512: (qi % 2) * 1024 + s_ * 512 + 512],
                    start=True, stop=True, skip_group_check=True),
                    waits=sw if half == 0 else [],
                    inc=("ten", 1) if half == 1 else None)
            smm = at("ten")
            last_smm_of_j[qi] = smm
            if pull:
                pull(2)
            if gp >= 1:
                emit_av(gp - 1)
                pgi, _, _, pp, pnp = pair_list[gp - 1]
                if pp == pnp - 1:
                    emit_epilogue(gp - 1, pull)
            if p == 0:
                gw = [("dgd", at("dgd"))] if gp == 0 else [("vec", scs_done)]
                g4_done = W("sync", lambda e, s_=s_, qi=qi: e.dma_start(
                    out=g_sb[:, :], in_=g_dram[s_:s_ + 1, qi * 512:(qi + 1) * 512]),
                    waits=gw, inc=("dg4", 16))
            tw = [("ten", smm)]
            if gp >= 1:
                tw.append(("sca", exp_done[gp - 1]))
            tanh_done[gp] = W("scalar", lambda e: e.activation(
                tts[:, :], ps_s[:, :], AF.Tanh, scale=TANH_SCALE),
                waits=tw, inc=("sca", 1))
            ew = [("sca", tanh_done[gp])]
            if gp >= 2:
                ew.append(("ten", av_read[gp - 2]))
                ew.append(("vec", dacc_done[gp - 2]))
            exp_done[gp] = W("scalar", lambda e, pslot=pslot: e.activation(
                pts(pslot), tts[:, :], AF.Exp, scale=EXP_SCALE),
                waits=ew, inc=("sca", 1))
            if diag:
                for half in range(2):
                    jj = 2 * (p - (npairs - 2)) + half
                    W("vector", lambda e, pslot=pslot, half=half, jj=jj: e.tensor_tensor(
                        pts(pslot)[:, half * 512:(half + 1) * 512],
                        pts(pslot)[:, half * 512:(half + 1) * 512],
                        cst[:, 512 + jj * 512: 1024 + jj * 512], op=ALU.mult),
                        waits=([("sca", exp_done[gp])] if half == 0 else [("vec", at("vec"))]),
                        inc=("vec", 1))
            dv = [("vec", at("vec"))]
            if not diag:
                dv.append(("sca", exp_done[gp]))
            if p == 0:
                if last_dbc:
                    dv.append(("vec", last_dbc))
                dacc_done[gp] = W("vector", lambda e, pslot=pslot: e.tensor_copy(
                    dacc[:, :], pts(pslot)[:, 0:512]),
                    waits=dv, inc=("vec", 1))
            else:
                dacc_done[gp] = W("vector", lambda e, pslot=pslot: e.tensor_tensor(
                    dacc[:, :], dacc[:, :], pts(pslot)[:, 0:512], op=ALU.add),
                    waits=dv, inc=("vec", 1))
            dacc_done[gp] = W("vector", lambda e, pslot=pslot: e.tensor_tensor(
                dacc[:, :], dacc[:, :], pts(pslot)[:, 512:1024], op=ALU.add),
                waits=[("vec", dacc_done[gp])], inc=("vec", 1))

        tsb = lambda slot: A5[:, slot * 2048:(slot + 1) * 2048]
        wps = lambda slot: A5[:, 4096 + slot * 2048: 4096 + (slot + 1) * 2048]
        tf_load_done = [0, 0]
        tf_wps_dma = [0, 0]
        tf_mm_done_k = {}
        tf_wp_count = {}
        tf_ev = {}

        def tf_chunks(kt_):
            slot = kt_ % 2

            def c_load():
                lw = []
                if kt_ < 2:
                    lw.append(("ten", g_done[15]))
                else:
                    lw.append(("ten", tf_mm_done_k[kt_ - 2]))
                tf_load_done[slot] = W("gpsimd", lambda e: e.dma_start(
                    out=tsb(slot), in_=w2tp[kt_ * 128:(kt_ + 1) * 128, :]),
                    waits=lw, inc=(f"dwt{slot}", 16))

            def c_mm(j0):
                for j in (j0, j0 + 1):
                    h = j % 2
                    mw = guard_waits(ps2_guard[h])
                    if j == 0:
                        mw.append((f"dwt{slot}", tf_load_done[slot]))
                    mm = W("tensor", lambda e, j=j, h=h: e.matmul(
                        ps2[:, h * 512:(h + 1) * 512], cst[:, 256:384],
                        tsb(slot)[:, j * 512:(j + 1) * 512],
                        start=True, stop=True, skip_group_check=True),
                        waits=mw, inc=("ten", 1))
                    ew = [("ten", mm)]
                    if j < 2 and tf_wps_dma[slot]:
                        ew.append((f"dwp{slot}", tf_wps_dma[slot]))
                    ev = W("vector", lambda e, j=j, h=h: e.tensor_scalar(
                        wps(slot)[:, j * 512:(j + 1) * 512],
                        ps2[:, h * 512:(h + 1) * 512], fcb[:, 2:3], None,
                        op0=ALU.mult),
                        waits=ew, inc=("vec", 1))
                    ps2_guard[h].append(("vec", ev))
                    tf_ev[(kt_, j)] = ev
                if j0 == 2:
                    tf_mm_done_k[kt_] = at("ten")
                    tf_wps_dma[slot] = W("sync", lambda e: e.dma_start(
                        out=w2p_dram[kt_ * 128:(kt_ + 1) * 128, :], in_=wps(slot)),
                        waits=[("vec", tf_ev[(kt_, j)]) for j in range(4)],
                        inc=(f"dwp{slot}", 16))
                    tf_wp_count[kt_] = tf_wps_dma[slot]

            return [(0, c_load), (2, lambda: c_mm(0)), (2, lambda: c_mm(2))]

        z2b = lambda slot: A4[:, 8192 + slot * 512: 8192 + (slot + 1) * 512]
        w2rb = lambda slot: A4[:, 4096 + slot * 512: 4096 + (slot + 1) * 512]
        BD = 8
        z2_done = [0] * BD
        w2r_done = [0] * BD

        def b_preload_chunks():
            def c_pre():
                for kp in range(BD):
                    zw = [("ten", g_done[15]),
                          ("dz0", zdma_done[2 * kp]), ("dz1", zdma_done[2 * kp + 1])]
                    z2_done[kp] = W("sync", lambda e, kp=kp: e.dma_start(
                        out=z2b(kp),
                        in_=z_dram[2 * kp:2 * kp + 2, :, 0:512]
                        .rearrange("s a t -> (s a) t")),
                        waits=zw, inc=(f"dz2{kp}", 16))
                    ww = [("ten", g_done[15]), (f"dwp{kp % 2}", tf_wp_count[kp])]
                    w2r_done[kp] = W("scalar", lambda e, kp=kp: e.dma_start(
                        out=w2rb(kp),
                        in_=w2p_dram[kp * 128:(kp + 1) * 128, 0:512]),
                        waits=ww, inc=(f"dw2{kp}", 16))
            return [(0, c_pre)]

        emit_w13_load(0)
        emit_w13_load(1)
        for u in range(9):
            for _c, _f in p3_chunks(0, u):
                _f()
        total_pairs = len(pair_list)
        sig_next = 0
        p3_units = deque()
        cur = deque()
        p3_pending = [0]

        def p2_rem():
            return (sum(c for c, _f, p3f in cur if not p3f)
                    + (128 - sig_next) * 33)

        def p3_rem():
            return (sum(sum(c for c, _f in u) for u in p3_units)
                    + sum(c for c, _f, p3f in cur if p3f))

        def pull(nmm):
            nonlocal sig_next
            got = 0
            while got < nmm:
                if not cur:
                    if p3_units:
                        cur.extend((c, f, True) for c, f in p3_units.popleft())
                        p3_pending[0] += len(cur)
                    elif sig_next < 128:
                        cur.extend((c, f, False) for c, f in p2_chunks(sig_next))
                        sig_next += 1
                    else:
                        break
                c, f, p3f = cur.popleft()
                if p3f:
                    p3_pending[0] -= 1
                f()
                got += c
            return got

        pairs_done = 0
        acc = 0.0
        for j in range(16):
            if j < 15:
                for u in range(9):
                    p3_units.append(p3_chunks(j + 1, u))
            else:
                for kt_i in range(64):
                    p3_units.append(tf_chunks(kt_i))
                    if kt_i == 7 and USE_B_PRELOAD:
                        p3_units.append(b_preload_chunks())
                    if kt_i % 4 == 3 and sig_next < 128:
                        p3_units.append(p2_chunks(sig_next))
                        sig_next += 1
            for pi in pairs_by_j[j]:
                pairs_rem = total_pairs - pairs_done
                quota = (p2_rem() + p3_rem()) / float(pairs_rem)
                acc += quota
                budget = [int(acc)]
                acc -= int(acc)

                def bpull(n, budget=budget):
                    got = pull(min(n, budget[0]))
                    budget[0] -= got
                emit_pair(pi, bpull)
                pull(budget[0])
                pairs_done += 1
            while p3_units or p3_pending[0]:
                pull(1)
        pull(10 ** 9)
        emit_av(total_pairs - 1)
        emit_epilogue(total_pairs - 1)
        z_unit(NSTEP - 2)
        z_unit(NSTEP - 1)
        z_all = (("dz0", at("dz0")), ("dz1", at("dz1")))

        if debug:
            W("sync", lambda e: e.dma_start(
                out=dbg["dbg_z"].rearrange("p (a t) -> p a t", a=64),
                in_=z_dram[:, :, :]),
                waits=list(z_all), inc=("dmisc", 16))
            W("sync", lambda e: e.dma_start(out=dbg["dbg_kt"], in_=kt[:, :]),
              waits=[("vec", at("vec"))], inc=("dmisc", 16))
            W("sync", lambda e: e.dma_start(out=dbg["dbg_vt"], in_=vt[:, :]), inc=("dmisc", 16))
            W("sync", lambda e: e.dma_start(out=dbg["dbg_g"], in_=g_dram[:, :]),
              waits=[("dgd", at("dgd"))], inc=("dmisc", 16))

        p4_vec_end = at("vec")
        p4_sca_end = at("sca")
        p3_ten_end = at("ten")

        def a2a_fn(e):
            with allow_ici_all_to_all():
                return e.collective_compute(
                    "AllToAll", ALU.bypass, replica_groups=[list(range(NCORE))],
                    ins=[a2a_in.ap().opt()], outs=[a2a_out.ap().opt()])
        W("gpsimd", a2a_fn,
          waits=[("dog0", last_ogdma[0]), ("dog1", last_ogdma[1])], inc=("cc2", 1))
        if debug:
            W("sync", lambda e: e.dma_start(out=dbg["dbg_a2a"], in_=a2a_out[:, :]),
              waits=[("cc2", 1)], inc=("dmisc", 16))

        afT = lambda slot, i: AFl[:, i * 512:(i + 1) * 512]

        if debug:
            W("sync", lambda e: e.dma_start(out=dbg["dbg_w2p"], in_=w2p_dram[:, :]),
              waits=[("dwp0", at("dwp0")), ("dwp1", at("dwp1"))], inc=("dmisc", 16))

        b_psum = [ps2[:, 0:512], ps2[:, 512:1024], pbA[:, :], pbB[:, :]]
        bmm_done = {}
        bev_done = {}
        afslot_adds = [0, 0]
        last_add_AT = [[None] * 4, [None] * 4]
        p6_units = [(ds, tau) for ds in range(4) for tau in range(2)]
        first_bmm = True
        for ui, (ds, tau) in enumerate(p6_units):
            for kp in range(64):
                slot = kp % BD
                if not (USE_B_PRELOAD and ui == 0 and kp < BD):
                    zw = []
                    prev = bmm_done.get((ui, kp - BD))
                    if prev is None and ui > 0 and kp < BD:
                        prev = bmm_done.get((ui - 1, 64 - BD + kp))
                    if prev:
                        zw.append(("ten", prev))
                    if ui == 0:
                        if kp < BD:
                            zw.append(("ten", g_done[15]))
                        zw.append(("dz0", zdma_done[2 * kp]))
                        zw.append(("dz1", zdma_done[2 * kp + 1]))
                    z2_done[slot] = W("sync", lambda e, kp=kp, tau=tau, slot=slot: e.dma_start(
                        out=z2b(slot),
                        in_=z_dram[2 * kp:2 * kp + 2, :, tau * 512:(tau + 1) * 512]
                        .rearrange("s a t -> (s a) t")),
                        waits=zw, inc=(f"dz2{slot}", 16))
                    ww = [(f"dwp{kp % 2}", tf_wp_count[kp])] if ui == 0 else []
                    if ui == 0 and kp < BD:
                        ww.append(("ten", g_done[15]))
                    if prev:
                        ww.append(("ten", prev))
                    w2r_done[slot] = W("scalar", lambda e, kp=kp, ds=ds, slot=slot: e.dma_start(
                        out=w2rb(slot),
                        in_=w2p_dram[kp * 128:(kp + 1) * 128, ds * 512:(ds + 1) * 512]),
                        waits=ww, inc=(f"dw2{slot}", 16))
                mw = [(f"dz2{slot}", z2_done[slot]), (f"dw2{slot}", w2r_done[slot])]
                if kp == 0:
                    if first_bmm:
                        mw.append(("vec", at("vec")))
                        mw.append(("sca", at("sca")))
                        first_bmm = False
                    if ui >= 1:
                        mw.append(("vec", max(bev_done[(ui - 1, i)] for i in range(4))))
                for T in range(4):
                    r_ = W("tensor", lambda e, T=T, slot=slot, kp=kp: e.matmul(
                        b_psum[T], z2b(slot)[:, T * 128:(T + 1) * 128], w2rb(slot),
                        start=(kp == 0), stop=(kp == 63), skip_group_check=True),
                        waits=mw if T == 0 else [], inc=("ten", 1) if T == 3 else None)
                    if T == 3:
                        bmm_done[(ui, kp)] = r_
            afs = ui % 2
            for T in range(4):
                ew = [("ten", bmm_done[(ui, 63)])]
                if last_add_AT[afs][T] is not None:
                    ew.append(last_add_AT[afs][T])
                bev_done[(ui, T)] = W("vector", lambda e, T=T, afs=afs: e.tensor_copy(
                    afT(afs, T), b_psum[T]),
                    waits=ew, inc=("vec", 1))
            for T in range(4):
                aw = [("vec", bev_done[(ui, T)])]
                if ui == 0 and T == 0:
                    aw.append(("dxcp", xcp_done))
                afslot_adds[afs] = W("gpsimd", lambda e, T=T, tau=tau, ds=ds, afs=afs: e.dma_start(
                    out=out[tau * 512 + T * 128: tau * 512 + T * 128 + 128,
                            ds * 512:(ds + 1) * 512],
                    in_=afT(afs, T), accum_op=ALU.add),
                    waits=aw, inc=(f"dac{afs}", 16))
                last_add_AT[afs][T] = (f"dac{afs}", afslot_adds[afs])
        p6_adds = (("dac0", at("dac0")), ("dac1", at("dac1")))

        PD = 8
        wo3 = lambda dslot: A3[:, dslot * 8192:(dslot + 1) * 8192]
        og4 = lambda slot: A4[:, slot * 512:(slot + 1) * 512]
        og_done = [0] * PD
        wo_cnt = {}
        pmm_done = {}
        pev_done = {}
        p5_units = [(ds, tcg) for ds in range(4) for tcg in range(2)]
        last_av_ten = av_read[total_pairs - 1]
        for ds0 in (0, 1):
            wo_cnt[ds0] = W("gpsimd", lambda e, ds0=ds0: e.dma_start(
                out=wo3(ds0).rearrange("p (hc d) -> p hc d", hc=16),
                in_=wout[:, ds0 * 512:(ds0 + 1) * 512]
                .rearrange("(hc p) d -> p hc d", p=128)),
                waits=[("ten", last_av_ten)], inc=(f"dwo{ds0}", 16))
        first_p5 = True
        for uix, (ds, tcg) in enumerate(p5_units):
            dslot = ds % 2
            if tcg == 0 and ds >= 2:
                wo_cnt[ds] = W("gpsimd", lambda e, ds=ds, dslot=dslot: e.dma_start(
                    out=wo3(dslot).rearrange("p (hc d) -> p hc d", hc=16),
                    in_=wout[:, ds * 512:(ds + 1) * 512]
                    .rearrange("(hc p) d -> p hc d", p=128)),
                    waits=[("ten", pmm_done[((ds - 2) * 2 + 1, 15)])],
                    inc=(f"dwo{dslot}", 16))
            for hc in range(16):
                slot = hc % PD
                prev = pmm_done.get((uix, hc - PD))
                if prev is None and uix > 0 and hc < PD:
                    prev = pmm_done.get((uix - 1, 16 - PD + hc))
                ogw = [("cc2", 1)] if first_p5 else []
                if prev:
                    ogw.append(("ten", prev))
                og_done[slot] = W("sync", lambda e, hc=hc, tcg=tcg, slot=slot: e.dma_start(
                    out=og4(slot),
                    in_=a2a_out[(hc % 8) * 256 + (hc // 8) * 128:
                                (hc % 8) * 256 + (hc // 8) * 128 + 128,
                                tcg * 512:(tcg + 1) * 512]),
                    waits=ogw, inc=(f"dog4{slot}", 16))
                first_p5 = False
                mw = [(f"dog4{slot}", og_done[slot])]
                if hc == 0:
                    mw.append((f"dwo{dslot}", wo_cnt[ds]))
                    if uix == 0:
                        mw.append(("vec", max(bev_done[(7, i)] for i in range(4))))
                    else:
                        mw.append(("vec", max(pev_done[(uix - 1, i)] for i in range(4))))
                for T in range(4):
                    r_ = W("tensor", lambda e, T=T, slot=slot, hc=hc, dslot=dslot: e.matmul(
                        b_psum[T], og4(slot)[:, T * 128:(T + 1) * 128],
                        wo3(dslot)[:, hc * 512:(hc + 1) * 512],
                        start=(hc == 0), stop=(hc == 15), skip_group_check=True),
                        waits=mw if T == 0 else [], inc=("ten", 1) if T == 3 else None)
                    if T == 3:
                        pmm_done[(uix, hc)] = r_
            afs = uix % 2
            for T in range(4):
                ew = [("ten", pmm_done[(uix, 15)])]
                if last_add_AT[afs][T] is not None:
                    ew.append(last_add_AT[afs][T])
                if uix < 2:
                    ew.extend(p6_adds)
                pev_done[(uix, T)] = W("vector", lambda e, T=T, afs=afs: e.tensor_copy(
                    afT(afs, T), b_psum[T]),
                    waits=ew, inc=("vec", 1))
            for T in range(4):
                aw = [("vec", pev_done[(uix, T)])]
                afslot_adds[afs] = W("gpsimd", lambda e, T=T, tcg=tcg, ds=ds, afs=afs: e.dma_start(
                    out=out[tcg * 512 + T * 128: tcg * 512 + T * 128 + 128,
                            ds * 512:(ds + 1) * 512],
                    in_=afT(afs, T), accum_op=ALU.add),
                    waits=aw, inc=(f"dac{afs}", 16))
                last_add_AT[afs][T] = (f"dac{afs}", afslot_adds[afs])

        W("gpsimd", lambda e: e.nop(),
          waits=[("dac0", at("dac0")), ("dac1", at("dac1")), ("dxcp", xcp_done)])
        W("sync", lambda e: e.nop(),
          waits=[("dac0", at("dac0")), ("dac1", at("dac1"))])

        with nc.Block() as block:
            def mk(ename):
                def emit(eng):
                    lastw = {}
                    for fn, waits, inc in sch.q[ename]:
                        for s_, v in waits:
                            if v > 0 and lastw.get(s_, -1) < v:
                                eng.wait_ge(sems[s_], v)
                                lastw[s_] = v
                        ins = fn(eng)
                        if inc is not None:
                            ins.then_inc(sems[inc[0]], inc[1])
                return emit
            block.tensor(mk("tensor"))
            block.scalar(mk("scalar"))
            block.vector(mk("vector"))
            block.gpsimd(mk("gpsimd"))
            block.sync(mk("sync"))

    return nc


def make_inputs(x, w_qkv, w_gates, w_out, W1, W2, W3):
    x = np.ascontiguousarray(np.asarray(x, np.float32).reshape(S, D))
    w_qkv = np.asarray(w_qkv, np.float32)
    w_gates = np.asarray(w_gates, np.float32)
    w_out = np.ascontiguousarray(np.asarray(w_out, np.float32))
    w1t = np.ascontiguousarray(np.asarray(W1, np.float32).T)
    w3t = np.ascontiguousarray(np.asarray(W3, np.float32).T)
    w2t = np.asarray(W2, np.float32).T
    w2tp = np.ascontiguousarray(w2t.reshape(64, 128, D).transpose(1, 0, 2).reshape(FF, D))
    cst = build_consts()
    fcv = np.zeros((128, 4), np.float32)
    fcv[:, 0] = EPS
    fcv[:, 1] = -1.0
    fcv[:, 2] = FWHT_SCALE * 0.5
    fcv[:, 3] = 1.0
    in_maps = []
    for c in range(NCORE):
        cols = []
        for kind in range(3):
            for s_ in range(2):
                h = c + 8 * s_
                cols.append(np.arange(kind * 2048 + h * 128, kind * 2048 + h * 128 + 128))
        cols = np.concatenate(cols)
        in_maps.append({
            "x": np.ascontiguousarray(x[c * TOWN:(c + 1) * TOWN]),
            "wqkv": np.ascontiguousarray(w_qkv[:, cols]),
            "wg": np.ascontiguousarray(w_gates[:, [c, c + 8]]),
            "wout": w_out,
            "w1t": w1t,
            "w3t": w3t,
            "w2tp": w2tp,
            "cst": cst,
            "fc": fcv,
        })
    return in_maps


def kernel(L0_attn_x, L0_attn_w_qkv, L0_attn_w_gates, L0_attn_w_out,
           L0_ffn_W1, L0_ffn_W2, L0_ffn_W3):
    in_maps = make_inputs(L0_attn_x, L0_attn_w_qkv, L0_attn_w_gates,
                          L0_attn_w_out, L0_ffn_W1, L0_ffn_W2, L0_ffn_W3)
    nc = build_nc()
    res = run_bass_kernel_spmd(nc, in_maps, core_ids=list(range(NCORE)))
    out = np.concatenate([np.asarray(res.results[c]["out"], np.float32)
                          for c in range(NCORE)], axis=0)
    return out.reshape(1, S, D)


if __name__ == "__main__":
    import time
    t0 = time.time()
    build_nc(debug=False)
    print(f"build ok in {time.time() - t0:.1f}s")


# version 25 (final)
# speedup vs baseline: 1.0101x; 1.0075x over previous
from collections import deque
from contextlib import ExitStack

import numpy as np
import ml_dtypes

import concourse.bass as bass
import concourse.mybir as mybir
from concourse.bass_utils import run_bass_kernel_spmd
from concourse.aot_env import allow_ici_all_to_all

F32 = mybir.dt.float32
BF16 = mybir.dt.bfloat16
AF = mybir.ActivationFunctionType
ALU = mybir.AluOpType

NCORE = 8
S = 8192
D = 2048
DH = 128
FF = 8192
TOWN = 1024
NTC = 16
TANH_SCALE = float((128.0 ** -0.5) / 50.0)
EXP_SCALE = 50.0
FWHT_SCALE = float(2.0 ** -6.5)
EPS = 1e-5

DEBUG = False
USE_B_PRELOAD = False

ENGS = ("tensor", "scalar", "vector", "gpsimd", "sync")


class Sched:
    def __init__(self):
        self.q = {e: [] for e in ENGS}
        self.cnt = {}

    def op(self, eng, fn, waits=(), inc=None):
        if inc is not None:
            s, amt = inc
            self.cnt[s] = self.cnt.get(s, 0) + amt
        self.q[eng].append((fn, tuple(waits), inc))
        return self.cnt[inc[0]] if inc is not None else None

    def at(self, s):
        return self.cnt.get(s, 0)


def _hadamard(n):
    h = np.array([[1.0]], np.float32)
    while h.shape[0] < n:
        h = np.block([[h, h], [h, -h]])
    return h


def build_consts():
    c = np.zeros((128, 2688), np.float32)
    c[:, 0:128] = np.eye(128)
    c[:, 128:256] = _hadamard(128)
    c[:, 256:384] = np.kron(np.eye(2), _hadamard(64))
    c[:, 384:512] = 1.0
    ii = np.arange(512)[None, :]
    pp = np.arange(128)[:, None]
    for jj in range(4):
        c[:, 512 + jj * 512: 1024 + jj * 512] = (jj * 128 + pp <= ii).astype(np.float32)
    c[:, 2560:2561] = 1.0
    c[:, 2561:2562] = 2.0
    return c.astype(ml_dtypes.bfloat16)


def build_nc(debug=DEBUG):
    nc = bass.Bass("TRN2", target_bir_lowering=False, debug=False)

    x_in = nc.dram_tensor("x", [TOWN, D], F32, kind="ExternalInput").ap()
    wqkv = nc.dram_tensor("wqkv", [D, 768], F32, kind="ExternalInput").ap()
    wg = nc.dram_tensor("wg", [D, 2], F32, kind="ExternalInput").ap()
    wout = nc.dram_tensor("wout", [D, D], F32, kind="ExternalInput").ap()
    w1t = nc.dram_tensor("w1t", [D, FF], F32, kind="ExternalInput").ap()
    w3t = nc.dram_tensor("w3t", [D, FF], F32, kind="ExternalInput").ap()
    w2tp = nc.dram_tensor("w2tp", [FF, D], F32, kind="ExternalInput").ap()
    cst_in = nc.dram_tensor("cst", [128, 2688], BF16, kind="ExternalInput").ap()
    fc_in = nc.dram_tensor("fc", [128, 4], F32, kind="ExternalInput").ap()
    out = nc.dram_tensor("out", [TOWN, D], F32, kind="ExternalOutput").ap()

    ag1_in = nc.dram_tensor("ag1_in", [D, TOWN], BF16)
    ag1_out = nc.dram_tensor("ag1_out", [NCORE * D, TOWN], BF16, addr_space="Shared")
    a2a_in = nc.dram_tensor("a2a_in", [2048, TOWN], BF16)
    a2a_out = nc.dram_tensor("a2a_out", [2048, TOWN], BF16)
    z_dram = nc.dram_tensor("z_dram", [128, 64, TOWN], BF16)
    w2p_dram = nc.dram_tensor("w2p_dram", [FF, D], BF16)
    g_dram = nc.dram_tensor("g_dram", [2, S], F32)

    dbg = {}
    if debug:
        for name, shape, dt in [
            ("dbg_xnt", [128, 16 * TOWN], BF16),
            ("dbg_kt", [128, 2 * S], BF16),
            ("dbg_vt", [128, 2 * S], BF16),
            ("dbg_g", [2, S], F32),
            ("dbg_z", [128, 64 * TOWN], BF16),
            ("dbg_w2p", [FF, D], BF16),
            ("dbg_a2a", [2048, TOWN], BF16),
        ]:
            dbg[name] = nc.dram_tensor(name, shape, dt, kind="ExternalOutput").ap()

    sch = Sched()

    with ExitStack() as st:
        sb = lambda n, s_, d: st.enter_context(nc.sbuf_tensor(n, s_, d))
        psT = lambda n, s_, d: st.enter_context(nc.psum_tensor(n, s_, d))

        cst = sb("cst_sb", [128, 2688], BF16)
        fcb = sb("fcb", [128, 4], F32)
        A1 = sb("A1", [128, 16384], BF16)
        A2 = sb("A2", [128, 12288], BF16)
        A3 = sb("A3", [128, 16384], BF16)
        A4 = sb("A4", [128, 12288], BF16)
        A5 = sb("A5", [128, 12800], BF16)
        KT = sb("KT", [128, 16384], BF16)
        qsm = sb("qsm", [128, 2048], BF16)
        AFl = sb("AF", [128, 2048], F32)
        vtmp = sb("vtmp", [128, 1024], BF16)
        wgb = sb("wgb", [128, 32], BF16)
        bn24 = sb("bn24", [128, 24], F32)
        mv = sb("mv", [128, 2], F32)
        nm = sb("nm", [128, 1], F32)
        sq = sb("sq", [128, 1], F32)
        rs = sb("rs", [128, 1], F32)
        g2sb = sb("g2sb", [2, 512], F32)
        g_sb = sb("g_sb", [1, 512], F32)
        rd_sb = sb("rd_sb", [1, 512], F32)
        scs = sb("scs", [1, 512], BF16)
        bcs = sb("bcs", [128, 512], BF16)
        dacc = sb("dacc", [128, 512], F32)
        dbc = sb("dbc", [128, 512], BF16)
        gt_sb = sb("gt_sb", [1, 512], F32)

        ps_s = psT("ps_s", [128, 1024], F32)
        ps2 = psT("ps2", [128, 1024], F32)
        pbA = psT("pbA", [128, 512], F32)
        pbB = psT("pbB", [128, 512], F32)
        pTa = psT("pTa", [128, 1024], BF16)
        pTb = psT("pTb", [128, 1024], BF16)

        sem_names = (
            ["ten", "sca", "vec", "cc1", "cc2", "dmisc", "dfc", "dgw", "dagi",
             "dxf0", "dxf1", "dw0", "dw1", "dz0", "dz1",
             "dxa0", "dxa1", "dgd", "dg4", "dog0", "dog1",
             "dwt0", "dwt1", "dwp0", "dwp1", "dac0", "dac1", "dxcp"]
            + [f"dz2{i}" for i in range(8)] + [f"dw2{i}" for i in range(8)]
            + [f"dwo{i}" for i in range(8)] + [f"dog4{i}" for i in range(8)]
        )
        sems = {n: st.enter_context(nc.semaphore(n)) for n in sem_names}

        W = sch.op
        at = sch.at

        pset = lambda i: ps_s if i % 2 == 0 else ps2

        xcp_done = W("sync", lambda e: e.dma_start(out=out, in_=x_in), inc=("dxcp", 16))
        cst_done = W("sync", lambda e: e.dma_start(out=cst[:, :], in_=cst_in), inc=("dmisc", 16))
        fc_done = W("sync", lambda e: e.dma_start(out=fcb[:, :], in_=fc_in), inc=("dfc", 16))
        wqkvb = A4
        W("gpsimd", lambda e: e.dma_start(
            out=wqkvb[:, 0:12288].rearrange("p (dc m) -> p dc m", dc=16),
            in_=wqkv.rearrange("(dc p) m -> p dc m", p=128)), inc=("dgw", 16))
        wg_done = W("gpsimd", lambda e: e.dma_start(
            out=wgb[:, :].rearrange("p (dc g) -> p dc g", dc=16),
            in_=wg.rearrange("(dc p) g -> p dc g", p=128)), inc=("dgw", 16))

        xnb = KT
        xnT = A1
        apply_done = [0] * 8
        prev_sqv = 0
        prev_bnv = 0
        for tt in range(8):
            slot = tt % 2
            waits = []
            if tt >= 1:
                waits.append(("vec", apply_done[tt - 1]))
            dxf = W("sync", lambda e, tt=tt: e.dma_start(
                out=AFl[:, 0:2048],
                in_=x_in[tt * 128:(tt + 1) * 128, :]),
                waits=waits, inc=(f"dxf{slot}", 16))
            xfv = lambda: AFl[:, 0:2048]
            for i in range(4):
                w_ = []
                if i == 0:
                    w_.append((f"dxf{slot}", dxf))
                    if tt > 0:
                        w_.append(("vec", prev_bnv))
                W("vector", lambda e, i=i: e.bn_stats(
                    bn24[:, 6 * i:6 * i + 6], xfv()[:, 512 * i:512 * i + 512]),
                    waits=w_, inc=("vec", 1))
            bw = [("vec", at("vec"))]
            if tt > 0:
                bw.append(("sca", prev_sqv))
            bnv = W("vector", lambda e: e.bn_aggr(mv[:, :], bn24[:, :]),
                    waits=bw, inc=("vec", 1))
            prev_bnv = bnv
            nmv = W("vector", lambda e: e.tensor_scalar(
                nm[:, :], mv[:, 0:1], fcb[:, 1:2], None, op0=ALU.mult),
                waits=[("vec", bnv), ("dfc", fc_done)], inc=("vec", 1))
            sqv = W("scalar", lambda e: e.activation(sq[:, :], mv[:, 1:2], AF.Sqrt,
                                                     bias=fcb[:, 0:1]),
                    waits=[("vec", bnv), ("dfc", fc_done)], inc=("sca", 1))
            prev_sqv = sqv
            rsv = W("vector", lambda e: e.reciprocal(rs[:, :], sq[:, :]),
                    waits=[("sca", sqv), ("vec", nmv)], inc=("vec", 1))
            apply_done[tt] = W("vector", lambda e, tt=tt: e.tensor_scalar(
                xnb[:, tt * 2048:(tt + 1) * 2048], xfv()[:, :], nm[:, 0:1], rs[:, 0:1],
                op0=ALU.add, op1=ALU.mult),
                waits=[("vec", rsv)], inc=("vec", 1))

        ln_all = apply_done[7]
        tp_done = [0, 0]
        ev_done = [0, 0]
        for dc in range(16):
            slot = dc % 2
            pT = pTa if slot == 0 else pTb
            for tt in range(8):
                w_ = []
                if dc == 0 and tt == 0:
                    w_ = [("vec", ln_all), ("dmisc", cst_done)]
                if tt == 0 and dc >= 2:
                    w_.append(("vec", ev_done[slot]))
                r_ = W("tensor", lambda e, dc=dc, tt=tt, pT=pT: e.transpose(
                    pT[:, tt * 128:(tt + 1) * 128],
                    xnb[:, tt * 2048 + dc * 128: tt * 2048 + dc * 128 + 128],
                    cst[:, 0:128]),
                    waits=w_, inc=("ten", 1) if tt == 7 else None)
                if tt == 7:
                    tp_done[slot] = r_
            ev_done[slot] = W("vector", lambda e, dc=dc, pT=pT: e.tensor_copy(
                xnT[:, dc * 1024:(dc + 1) * 1024], pT[:, :]),
                waits=[("ten", tp_done[slot])], inc=("vec", 1))
        xnt_ready_all = max(ev_done)

        agi = 0
        for dc in range(16):
            agi = W("sync", lambda e, dc=dc: e.dma_start(
                out=ag1_in[dc * 128:(dc + 1) * 128, :],
                in_=xnT[:, dc * 1024:(dc + 1) * 1024]),
                waits=[("vec", ev_done[dc % 2])], inc=("dagi", 16))
        W("gpsimd", lambda e: e.collective_compute(
            "AllGather", ALU.bypass, replica_groups=[list(range(NCORE))],
            ins=[ag1_in.ap().opt()], outs=[ag1_out.ap().opt()]),
            waits=[("dagi", agi)], inc=("cc1", 1))

        if debug:
            W("sync", lambda e: e.dma_start(out=dbg["dbg_xnt"], in_=xnT[:, :]),
              waits=[("vec", xnt_ready_all)], inc=("dmisc", 16))

        w13 = lambda slot: A2[:, slot * 4096:(slot + 1) * 4096]
        hbuf = lambda slot: A2[:, 8192 + slot * 512: 8192 + (slot + 1) * 512]
        stmp = lambda slot: A2[:, 9216 + slot * 512: 9216 + (slot + 1) * 512]
        zstage = lambda slot: A2[:, 10240 + slot * 512: 10240 + (slot + 1) * 512]
        s1b = lambda slot: A2[:, 11264 + slot * 512: 11264 + (slot + 1) * 512]

        kt, vt = KT, A3
        xadst = lambda slot: A5[:, 0:8192]
        tts = A5[:, 8192:9216]
        pts = lambda slot: A5[:, 9216 + slot * 1024: 9216 + (slot + 1) * 1024]
        og_stage = lambda slot: A5[:, 11264 + slot * 512: 11264 + (slot + 1) * 512]

        ps2_guard = [[], []]
        pbB_guard = []

        def guard_waits(g):
            w = list(g)
            del g[:]
            return w

        NSTEP = 128
        upmm_done = [0] * NSTEP
        tsil_done = [0] * NSTEP
        s1_done = [0] * NSTEP
        hm_done = [0] * NSTEP
        zmm_done = [0] * NSTEP
        zev_done = [0] * NSTEP
        zdma_done = [0] * NSTEP
        w13_done = [0] * 64

        def z_unit(sig):
            if sig < 0 or sig >= NSTEP:
                return
            chi, tau = divmod(sig, 2)
            hslot = sig % 2
            waits = [("vec", hm_done[sig])] + guard_waits(pbB_guard)
            zmm_done[sig] = W("tensor", lambda e, hslot=hslot: e.matmul(
                pbB[:, :], cst[:, 128:256], hbuf(hslot),
                start=True, stop=True, skip_group_check=True),
                waits=waits, inc=("ten", 1))
            zw = [("ten", zmm_done[sig])]
            if sig >= 2:
                zw.append((f"dz{hslot}", zdma_done[sig - 2]))
            zev_done[sig] = W("vector", lambda e, hslot=hslot: e.tensor_copy(
                zstage(hslot), pbB[:, :]),
                waits=zw, inc=("vec", 1))
            pbB_guard.append(("vec", zev_done[sig]))
            zdma_done[sig] = W("sync", lambda e, chi=chi, tau=tau, hslot=hslot: e.dma_start(
                out=z_dram[:, chi, tau * 512:(tau + 1) * 512], in_=zstage(hslot)),
                waits=[("vec", zev_done[sig])], inc=(f"dz{hslot}", 16))

        def emit_w13_load(chi):
            wslot = chi % 2
            w_ = []
            if chi >= 2:
                w_.append(("ten", upmm_done[2 * (chi - 2) + 1]))
            for wi, wt in ((0, w1t), (1, w3t)):
                w13_done[chi] = W("gpsimd", lambda e, chi=chi, wi=wi, wt=wt, wslot=wslot: e.dma_start(
                    out=w13(wslot)[:, wi * 2048:(wi + 1) * 2048].rearrange("p (dc f) -> p dc f", dc=16),
                    in_=wt[:, chi * 128:(chi + 1) * 128].rearrange("(dc p) f -> p dc f", p=128)),
                    waits=w_ if wi == 0 else [], inc=(f"dw{wslot}", 16))

        def p2_chunks(sig):
            chi, tau = divmod(sig, 2)
            wslot = chi % 2
            hslot = sig % 2

            def up_half(wi, half):
                ww = []
                if half == 0:
                    ww = guard_waits(ps2_guard[wi])
                    if wi == 0:
                        ww.append((f"dw{wslot}", w13_done[chi]))
                        if sig == 0:
                            ww.append(("vec", xnt_ready_all))
                for dc in range(half * 8, half * 8 + 8):
                    W("tensor", lambda e, wi=wi, dc=dc: e.matmul(
                        ps2[:, wi * 512:(wi + 1) * 512],
                        w13(wslot)[:, wi * 2048 + dc * 128: wi * 2048 + dc * 128 + 128],
                        xnT[:, dc * 1024 + tau * 512: dc * 1024 + tau * 512 + 512],
                        start=(dc == 0), stop=(dc == 15), skip_group_check=True),
                        waits=ww if dc == half * 8 else [],
                        inc=("ten", 1) if dc == 15 else None)

            def c_wi0b():
                up_half(0, 1)
                upg = at("ten")
                tsw = [("ten", upg)]
                if sig >= 2:
                    tsw.append(("vec", s1_done[sig - 2]))
                tsil_done[sig] = W("scalar", lambda e: e.activation(
                    stmp(hslot), ps2[:, 0:512], AF.Tanh, scale=0.5),
                    waits=tsw, inc=("sca", 1))
                s1w = [("sca", tsil_done[sig]), ("dfc", fc_done)]
                if sig >= 2:
                    s1w.append(("ten", zmm_done[sig - 2]))
                s1_done[sig] = W("vector", lambda e: e.scalar_tensor_tensor(
                    s1b(hslot), stmp(hslot), fcb[:, 3:4], ps2[:, 0:512],
                    op0=ALU.add, op1=ALU.mult),
                    waits=s1w, inc=("vec", 1))
                ps2_guard[0].extend([("sca", tsil_done[sig]), ("vec", s1_done[sig])])

            def c_wi1b():
                up_half(1, 1)
                upmm_done[sig] = at("ten")
                hw = [("vec", s1_done[sig]), ("ten", upmm_done[sig])]
                if sig >= 2:
                    hw.append(("ten", zmm_done[sig - 2]))
                hm_done[sig] = W("vector", lambda e: e.tensor_tensor(
                    hbuf(hslot), s1b(hslot), ps2[:, 512:1024], op=ALU.mult),
                    waits=hw, inc=("vec", 1))
                ps2_guard[1].append(("vec", hm_done[sig]))
                if tau == 1 and chi + 2 < 64:
                    emit_w13_load(chi + 2)

            return [(1, lambda: z_unit(sig - 2)),
                    (8, lambda: up_half(0, 0)), (8, c_wi0b),
                    (8, lambda: up_half(1, 0)), (8, c_wi1b)]

        m_evac_done = [[0] * 6 for _ in range(NTC)]
        g_done = [0] * NTC
        gev_done = [0] * NTC
        vtr_done = [0] * NTC
        vte_done = [0] * NTC
        xad_cnt = [0] * NTC
        gdma = 0

        def p3_chunks(tc, u):
            slot = tc % 2

            if u == 0:
                def c_dma():
                    r, wnd = divmod(tc, 2)
                    w_ = [("cc1", 1)]
                    if tc >= 1:
                        w_.append(("ten", g_done[tc - 1]))
                    xad_cnt[tc] = W("sync", lambda e, r=r, wnd=wnd: e.dma_start(
                        out=xadst(slot).rearrange("p (dc t) -> p dc t", dc=16),
                        in_=ag1_out[r * 2048:(r + 1) * 2048, wnd * 512:(wnd + 1) * 512]
                        .rearrange("(dc p) t -> p dc t", p=128)),
                        waits=w_, inc=(f"dxa{slot}", 16))
                return [(0, c_dma)]

            if 1 <= u <= 6:
                m = u - 1
                half = m % 2
                pdst = ps2[:, half * 512:(half + 1) * 512]

                def m_half(hh):
                    ww = []
                    if hh == 0:
                        ww = [(f"dxa{slot}", xad_cnt[tc])] + guard_waits(ps2_guard[half])
                        if tc == 0 and m == 0:
                            ww.append(("dgw", wg_done))
                    for dc in range(hh * 8, hh * 8 + 8):
                        W("tensor", lambda e, dc=dc: e.matmul(
                            pdst, wqkvb[:, dc * 768 + m * 128: dc * 768 + m * 128 + 128],
                            xadst(slot)[:, dc * 512:(dc + 1) * 512],
                            start=(dc == 0), stop=(dc == 15), skip_group_check=True),
                            waits=ww if dc == hh * 8 else [],
                            inc=("ten", 1) if dc == 15 else None)

                def c_mb():
                    m_half(1)
                    last = at("ten")
                    if m < 2:
                        dview = (lambda s_=m, tc=tc:
                                 qsm[:, (tc % 2) * 1024 + s_ * 512: (tc % 2) * 1024 + s_ * 512 + 512])
                    elif m < 4:
                        dview = (lambda s_=m - 2, tc=tc:
                                 kt[:, s_ * S + tc * 512: s_ * S + tc * 512 + 512])
                    else:
                        dview = (lambda s_=m - 4: vtmp[:, s_ * 512:(s_ + 1) * 512])
                    ew = [("ten", last)]
                    if tc == 0 and m == 0:
                        ew.append(("dagi", agi))
                    if m < 2 and tc >= 2:
                        ew.append(("ten", last_smm_of_j[tc - 2]))
                    if m >= 4 and tc >= 1:
                        ew.append(("ten", vtr_done[tc - 1]))
                    m_evac_done[tc][m] = W("vector", lambda e, dview=dview:
                                           e.tensor_copy(dview(), pdst),
                                           waits=ew, inc=("vec", 1))
                    ps2_guard[half].append(("vec", m_evac_done[tc][m]))
                return [(8, lambda: m_half(0)), (8, c_mb)]

            if u == 7:
                def g_half(hh):
                    ww = []
                    if hh == 0:
                        ww = [(f"dxa{slot}", xad_cnt[tc])] + guard_waits(ps2_guard[0])
                    for dc in range(hh * 8, hh * 8 + 8):
                        W("tensor", lambda e, dc=dc: e.matmul(
                            ps2[0:2, 0:512], wgb[:, dc * 2:(dc + 1) * 2],
                            xadst(slot)[:, dc * 512:(dc + 1) * 512],
                            start=(dc == 0), stop=(dc == 15), skip_group_check=True),
                            waits=ww if dc == hh * 8 else [],
                            inc=("ten", 1) if dc == 15 else None)

                def c_gb():
                    g_half(1)
                    g_done[tc] = at("ten")
                    gv = [("ten", g_done[tc])]
                    if tc >= 1:
                        gv.append(("dgd", sch.at("dgd")))
                    gev_done[tc] = W("vector", lambda e: e.tensor_copy(g2sb[:, :], ps2[0:2, 0:512]),
                                     waits=gv, inc=("vec", 1))
                    ps2_guard[0].append(("vec", gev_done[tc]))
                    for s_ in range(2):
                        W("gpsimd", lambda e, s_=s_: e.dma_start(
                            out=g_dram[s_:s_ + 1, tc * 512:(tc + 1) * 512], in_=g2sb[s_:s_ + 1, :]),
                            waits=[("vec", gev_done[tc])] if s_ == 0 else [], inc=("dgd", 16))
                return [(8, lambda: g_half(0)), (8, c_gb)]

            def c_vt():
                vw = [("vec", m_evac_done[tc][4]), ("vec", m_evac_done[tc][5])]
                if tc >= 1:
                    vw.append(("vec", vte_done[tc - 1]))
                for s_ in range(2):
                    for qq in range(4):
                        W("tensor", lambda e, s_=s_, qq=qq: e.transpose(
                            pTa[:, (s_ * 4 + qq) * 128:(s_ * 4 + qq) * 128 + 128],
                            vtmp[:, s_ * 512 + qq * 128: s_ * 512 + qq * 128 + 128],
                            cst[:, 0:128]),
                            waits=vw if (s_ == 0 and qq == 0) else [],
                            inc=("ten", 1) if (s_ == 1 and qq == 3) else None)
                vtr_done[tc] = at("ten")
                v1 = W("vector", lambda e, tc=tc: e.tensor_copy(
                    vt[:, tc * 512: tc * 512 + 512], pTa[:, 0:512]),
                    waits=[("ten", vtr_done[tc])], inc=("vec", 1))
                vte_done[tc] = W("vector", lambda e, tc=tc: e.tensor_copy(
                    vt[:, S + tc * 512: S + tc * 512 + 512], pTa[:, 512:1024]),
                    waits=[("vec", v1)], inc=("vec", 1))
            return [(3, c_vt)]

        pair_list = []
        pairs_by_j = {}
        gi = 0
        for qi in range(NTC):
            for s_ in range(2):
                npairs = 2 * (qi + 1)
                for p in range(npairs):
                    pairs_by_j.setdefault(qi, []).append(len(pair_list))
                    pair_list.append((gi, s_, qi, p, npairs))
                gi += 1

        g4_done = 0
        scs_done = 0
        last_smm_of_j = [0] * NTC
        tanh_done = {}
        exp_done = {}
        av_read = {}
        dacc_done = {}
        last_ogm = 0
        last_ogdma = [0, 0]
        last_rdv = 0
        last_bcev = 0
        last_bcmm = 0
        last_onesmm = 0
        last_dbc = 0
        last_gt = 0
        first_p4 = True

        def emit_av(gp):
            gi, s_, qi, p, npairs = pair_list[gp]
            pslot = gp % 2
            diag = (p >= npairs - 2)
            mw = [("vec", dacc_done[gp])] if diag else [("sca", exp_done[gp])]
            if p == 0 and last_ogm:
                mw.append(("vec", last_ogm))
            for half in range(2):
                jb = 2 * p + half
                W("tensor", lambda e, s_=s_, jb=jb, pslot=pslot, half=half, p=p,
                    npairs=npairs: e.matmul(
                    pbA[:, :], vt[:, s_ * S + jb * 128: s_ * S + jb * 128 + 128],
                    pts(pslot)[:, half * 512:(half + 1) * 512],
                    start=(p == 0 and half == 0), stop=(p == npairs - 1 and half == 1),
                    skip_group_check=True),
                    waits=mw if half == 0 else [], inc=("ten", 1) if half == 1 else None)
            av_read[gp] = at("ten")

        def emit_epilogue(gp, pull=None):
            nonlocal scs_done, last_rdv, last_bcev, last_bcmm, last_ogm
            nonlocal last_onesmm, last_dbc, last_gt
            gi, s_, qi, p, npairs = pair_list[gp]
            dw = [("vec", dacc_done[gp])]
            if last_onesmm:
                dw.append(("ten", last_onesmm))
            last_dbc = W("vector", lambda e: e.tensor_copy(dbc[:, :], dacc[:, :]),
                         waits=dw, inc=("vec", 1))
            if pull:
                pull(3)
            omw = [("vec", last_dbc)] + guard_waits(pbB_guard)
            last_onesmm = W("tensor", lambda e: e.matmul(
                pbB[0:1, :], cst[:, 2561:2562], dbc[:, :],
                start=True, stop=True, skip_group_check=True),
                waits=omw, inc=("ten", 1))
            gtw = [("dg4", g4_done)]
            last_gt = W("scalar", lambda e: e.activation(gt_sb[:, :], g_sb[:, :],
                                                         AF.Tanh, scale=0.5),
                        waits=gtw, inc=("sca", 1))
            rw = [("ten", last_onesmm)]
            last_rdv = W("vector", lambda e: e.reciprocal(rd_sb[:, :], pbB[0:1, :]),
                         waits=rw, inc=("vec", 1))
            scw = [("vec", last_rdv), ("sca", last_gt)]
            if last_bcmm:
                scw.append(("ten", last_bcmm))
            scs_done = W("vector", lambda e: e.scalar_tensor_tensor(
                scs[:, :], gt_sb[:, :], fcb[0:1, 3:4], rd_sb[:, :],
                op0=ALU.add, op1=ALU.mult),
                waits=scw, inc=("vec", 1))
            if pull:
                pull(3)
            last_bcmm = W("tensor", lambda e: e.matmul(
                pbB[:, :], cst[0:1, 384:512], scs[:, :],
                start=True, stop=True, skip_group_check=True),
                waits=[("vec", scs_done)], inc=("ten", 1))
            last_bcev = W("vector", lambda e: e.tensor_copy(bcs[:, :], pbB[:, :]),
                          waits=[("ten", last_bcmm)], inc=("vec", 1))
            pbB_guard.append(("vec", last_bcev))
            ogslot = gi % 2
            ogw = [("vec", last_bcev), ("ten", av_read[gp])]
            if last_ogdma[ogslot]:
                ogw.append((f"dog{ogslot}", last_ogdma[ogslot]))
            last_ogm = W("vector", lambda e, ogslot=ogslot: e.tensor_tensor(
                og_stage(ogslot), pbA[:, :], bcs[:, :], op=ALU.mult),
                waits=ogw, inc=("vec", 1))
            jdst, wdst = divmod(qi, 2)
            last_ogdma[ogslot] = W("sync", lambda e, jdst=jdst, wdst=wdst, s_=s_,
                                   ogslot=ogslot: e.dma_start(
                out=a2a_in[jdst * 256 + s_ * 128: jdst * 256 + s_ * 128 + 128,
                           wdst * 512:(wdst + 1) * 512],
                in_=og_stage(ogslot)),
                waits=[("vec", last_ogm)], inc=(f"dog{ogslot}", 16))

        def emit_pair(gp, pull=None):
            nonlocal first_p4, g4_done
            gi, s_, qi, p, npairs = pair_list[gp]
            pslot = gp % 2
            diag = (p >= npairs - 2)
            sw = []
            if first_p4:
                sw = [("dmisc", cst_done)]
                first_p4 = False
            if gp >= 1:
                sw.append(("sca", tanh_done[gp - 1]))
            if p == 0:
                sw.append(("vec", max(m_evac_done[qi])))
                sw.append(("vec", vte_done[qi]))
            for half in range(2):
                jb = 2 * p + half
                W("tensor", lambda e, s_=s_, qi=qi, jb=jb, half=half: e.matmul(
                    ps_s[:, half * 512:(half + 1) * 512],
                    kt[:, s_ * S + jb * 128: s_ * S + jb * 128 + 128],
                    qsm[:, (qi % 2) * 1024 + s_ * 512: (qi % 2) * 1024 + s_ * 512 + 512],
                    start=True, stop=True, skip_group_check=True),
                    waits=sw if half == 0 else [],
                    inc=("ten", 1) if half == 1 else None)
            smm = at("ten")
            last_smm_of_j[qi] = smm
            if pull:
                pull(2)
            if gp >= 1:
                emit_av(gp - 1)
                pgi, _, _, pp, pnp = pair_list[gp - 1]
                if pp == pnp - 1:
                    emit_epilogue(gp - 1, pull)
            if p == 0:
                gw = [("dgd", 32 * (qi + 1))]
                if gp > 0:
                    gw.append(("vec", scs_done))
                g4_done = W("sync", lambda e, s_=s_, qi=qi: e.dma_start(
                    out=g_sb[:, :], in_=g_dram[s_:s_ + 1, qi * 512:(qi + 1) * 512]),
                    waits=gw, inc=("dg4", 16))
            tw = [("ten", smm)]
            if gp >= 1:
                tw.append(("sca", exp_done[gp - 1]))
            tanh_done[gp] = W("scalar", lambda e: e.activation(
                tts[:, :], ps_s[:, :], AF.Tanh, scale=TANH_SCALE),
                waits=tw, inc=("sca", 1))
            ew = [("sca", tanh_done[gp])]
            if gp >= 2:
                ew.append(("ten", av_read[gp - 2]))
                ew.append(("vec", dacc_done[gp - 2]))
            exp_done[gp] = W("scalar", lambda e, pslot=pslot: e.activation(
                pts(pslot), tts[:, :], AF.Exp, scale=EXP_SCALE),
                waits=ew, inc=("sca", 1))
            if diag:
                for half in range(2):
                    jj = 2 * (p - (npairs - 2)) + half
                    W("vector", lambda e, pslot=pslot, half=half, jj=jj: e.tensor_tensor(
                        pts(pslot)[:, half * 512:(half + 1) * 512],
                        pts(pslot)[:, half * 512:(half + 1) * 512],
                        cst[:, 512 + jj * 512: 1024 + jj * 512], op=ALU.mult),
                        waits=([("sca", exp_done[gp])] if half == 0 else [("vec", at("vec"))]),
                        inc=("vec", 1))
            dv = [("vec", at("vec"))]
            if not diag:
                dv.append(("sca", exp_done[gp]))
            if p == 0:
                if last_dbc:
                    dv.append(("vec", last_dbc))
                dacc_done[gp] = W("vector", lambda e, pslot=pslot: e.tensor_copy(
                    dacc[:, :], pts(pslot)[:, 0:512]),
                    waits=dv, inc=("vec", 1))
            else:
                dacc_done[gp] = W("vector", lambda e, pslot=pslot: e.tensor_tensor(
                    dacc[:, :], dacc[:, :], pts(pslot)[:, 0:512], op=ALU.add),
                    waits=dv, inc=("vec", 1))
            dacc_done[gp] = W("vector", lambda e, pslot=pslot: e.tensor_tensor(
                dacc[:, :], dacc[:, :], pts(pslot)[:, 512:1024], op=ALU.add),
                waits=[("vec", dacc_done[gp])], inc=("vec", 1))

        tsb = lambda slot: A5[:, slot * 2048:(slot + 1) * 2048]
        wps = lambda slot: A5[:, 4096 + slot * 2048: 4096 + (slot + 1) * 2048]
        tf_load_done = [0, 0]
        tf_wps_dma = [0, 0]
        tf_mm_done_k = {}
        tf_wp_count = {}
        tf_ev = {}

        def tf_chunks(kt_):
            slot = kt_ % 2

            def c_load():
                lw = []
                if kt_ < 2:
                    lw.append(("ten", g_done[15]))
                else:
                    lw.append(("ten", tf_mm_done_k[kt_ - 2]))
                tf_load_done[slot] = W("gpsimd", lambda e: e.dma_start(
                    out=tsb(slot), in_=w2tp[kt_ * 128:(kt_ + 1) * 128, :]),
                    waits=lw, inc=(f"dwt{slot}", 16))

            def c_mm(j0):
                for j in (j0, j0 + 1):
                    h = j % 2
                    mw = guard_waits(ps2_guard[h])
                    if j == 0:
                        mw.append((f"dwt{slot}", tf_load_done[slot]))
                    mm = W("tensor", lambda e, j=j, h=h: e.matmul(
                        ps2[:, h * 512:(h + 1) * 512], cst[:, 256:384],
                        tsb(slot)[:, j * 512:(j + 1) * 512],
                        start=True, stop=True, skip_group_check=True),
                        waits=mw, inc=("ten", 1))
                    ew = [("ten", mm)]
                    if j < 2 and tf_wps_dma[slot]:
                        ew.append((f"dwp{slot}", tf_wps_dma[slot]))
                    ev = W("vector", lambda e, j=j, h=h: e.tensor_scalar(
                        wps(slot)[:, j * 512:(j + 1) * 512],
                        ps2[:, h * 512:(h + 1) * 512], fcb[:, 2:3], None,
                        op0=ALU.mult),
                        waits=ew, inc=("vec", 1))
                    ps2_guard[h].append(("vec", ev))
                    tf_ev[(kt_, j)] = ev
                if j0 == 2:
                    tf_mm_done_k[kt_] = at("ten")
                    tf_wps_dma[slot] = W("sync", lambda e: e.dma_start(
                        out=w2p_dram[kt_ * 128:(kt_ + 1) * 128, :], in_=wps(slot)),
                        waits=[("vec", tf_ev[(kt_, j)]) for j in range(4)],
                        inc=(f"dwp{slot}", 16))
                    tf_wp_count[kt_] = tf_wps_dma[slot]

            return [(0, c_load), (2, lambda: c_mm(0)), (2, lambda: c_mm(2))]

        z2b = lambda slot: A4[:, 8192 + slot * 512: 8192 + (slot + 1) * 512]
        w2rb = lambda slot: A4[:, 4096 + slot * 512: 4096 + (slot + 1) * 512]
        BD = 8
        z2_done = [0] * BD
        w2r_done = [0] * BD

        def b_preload_chunks():
            def c_pre():
                for kp in range(BD):
                    zw = [("ten", g_done[15]),
                          ("dz0", zdma_done[2 * kp]), ("dz1", zdma_done[2 * kp + 1])]
                    z2_done[kp] = W("sync", lambda e, kp=kp: e.dma_start(
                        out=z2b(kp),
                        in_=z_dram[2 * kp:2 * kp + 2, :, 0:512]
                        .rearrange("s a t -> (s a) t")),
                        waits=zw, inc=(f"dz2{kp}", 16))
                    ww = [("ten", g_done[15]), (f"dwp{kp % 2}", tf_wp_count[kp])]
                    w2r_done[kp] = W("scalar", lambda e, kp=kp: e.dma_start(
                        out=w2rb(kp),
                        in_=w2p_dram[kp * 128:(kp + 1) * 128, 0:512]),
                        waits=ww, inc=(f"dw2{kp}", 16))
            return [(0, c_pre)]

        emit_w13_load(0)
        emit_w13_load(1)
        for u in range(9):
            for _c, _f in p3_chunks(0, u):
                _f()
        total_pairs = len(pair_list)
        sig_next = 0
        p3_units = deque()
        cur = deque()
        p3_pending = [0]

        def p2_rem():
            return (sum(c for c, _f, p3f in cur if not p3f)
                    + (128 - sig_next) * 33)

        def p3_rem():
            return (sum(sum(c for c, _f in u) for u in p3_units)
                    + sum(c for c, _f, p3f in cur if p3f))

        def pull(nmm):
            nonlocal sig_next
            got = 0
            while got < nmm:
                if not cur:
                    if p3_units:
                        cur.extend((c, f, True) for c, f in p3_units.popleft())
                        p3_pending[0] += len(cur)
                    elif sig_next < 128:
                        cur.extend((c, f, False) for c, f in p2_chunks(sig_next))
                        sig_next += 1
                    else:
                        break
                c, f, p3f = cur.popleft()
                if p3f:
                    p3_pending[0] -= 1
                f()
                got += c
            return got

        pairs_done = 0
        acc = 0.0
        for j in range(16):
            if j < 15:
                for u in range(9):
                    p3_units.append(p3_chunks(j + 1, u))
            else:
                for kt_i in range(64):
                    p3_units.append(tf_chunks(kt_i))
                    if kt_i == 7 and USE_B_PRELOAD:
                        p3_units.append(b_preload_chunks())
                    if kt_i % 4 == 3 and sig_next < 128:
                        p3_units.append(p2_chunks(sig_next))
                        sig_next += 1
            for pi in pairs_by_j[j]:
                pairs_rem = total_pairs - pairs_done
                quota = (p2_rem() + p3_rem()) / float(pairs_rem)
                acc += quota
                budget = [int(acc)]
                acc -= int(acc)

                def bpull(n, budget=budget):
                    got = pull(min(n, budget[0]))
                    budget[0] -= got
                emit_pair(pi, bpull)
                pull(budget[0])
                pairs_done += 1
            while p3_units or p3_pending[0]:
                pull(1)
        pull(10 ** 9)
        emit_av(total_pairs - 1)
        emit_epilogue(total_pairs - 1)
        z_unit(NSTEP - 2)
        z_unit(NSTEP - 1)
        z_all = (("dz0", at("dz0")), ("dz1", at("dz1")))

        if debug:
            W("sync", lambda e: e.dma_start(
                out=dbg["dbg_z"].rearrange("p (a t) -> p a t", a=64),
                in_=z_dram[:, :, :]),
                waits=list(z_all), inc=("dmisc", 16))
            W("sync", lambda e: e.dma_start(out=dbg["dbg_kt"], in_=kt[:, :]),
              waits=[("vec", at("vec"))], inc=("dmisc", 16))
            W("sync", lambda e: e.dma_start(out=dbg["dbg_vt"], in_=vt[:, :]), inc=("dmisc", 16))
            W("sync", lambda e: e.dma_start(out=dbg["dbg_g"], in_=g_dram[:, :]),
              waits=[("dgd", at("dgd"))], inc=("dmisc", 16))

        p4_vec_end = at("vec")
        p4_sca_end = at("sca")
        p3_ten_end = at("ten")

        def a2a_fn(e):
            with allow_ici_all_to_all():
                return e.collective_compute(
                    "AllToAll", ALU.bypass, replica_groups=[list(range(NCORE))],
                    ins=[a2a_in.ap().opt()], outs=[a2a_out.ap().opt()])
        W("gpsimd", a2a_fn,
          waits=[("dog0", last_ogdma[0]), ("dog1", last_ogdma[1])], inc=("cc2", 1))
        if debug:
            W("sync", lambda e: e.dma_start(out=dbg["dbg_a2a"], in_=a2a_out[:, :]),
              waits=[("cc2", 1)], inc=("dmisc", 16))

        afT = lambda slot, i: AFl[:, i * 512:(i + 1) * 512]

        if debug:
            W("sync", lambda e: e.dma_start(out=dbg["dbg_w2p"], in_=w2p_dram[:, :]),
              waits=[("dwp0", at("dwp0")), ("dwp1", at("dwp1"))], inc=("dmisc", 16))

        b_psum = [ps2[:, 0:512], ps2[:, 512:1024], pbA[:, :], pbB[:, :]]
        bmm_done = {}
        bev_done = {}
        afslot_adds = [0, 0]
        last_add_AT = [[None] * 4, [None] * 4]
        p6_units = [(ds, tau) for ds in range(4) for tau in range(2)]
        first_bmm = True
        for ui, (ds, tau) in enumerate(p6_units):
            for kp in range(64):
                slot = kp % BD
                if not (USE_B_PRELOAD and ui == 0 and kp < BD):
                    zw = []
                    prev = bmm_done.get((ui, kp - BD))
                    if prev is None and ui > 0 and kp < BD:
                        prev = bmm_done.get((ui - 1, 64 - BD + kp))
                    if prev:
                        zw.append(("ten", prev))
                    if ui == 0:
                        if kp < BD:
                            zw.append(("ten", g_done[15]))
                        zw.append(("dz0", zdma_done[2 * kp]))
                        zw.append(("dz1", zdma_done[2 * kp + 1]))
                    z2_done[slot] = W("sync", lambda e, kp=kp, tau=tau, slot=slot: e.dma_start(
                        out=z2b(slot),
                        in_=z_dram[2 * kp:2 * kp + 2, :, tau * 512:(tau + 1) * 512]
                        .rearrange("s a t -> (s a) t")),
                        waits=zw, inc=(f"dz2{slot}", 16))
                    ww = [(f"dwp{kp % 2}", tf_wp_count[kp])] if ui == 0 else []
                    if ui == 0 and kp < BD:
                        ww.append(("ten", g_done[15]))
                    if prev:
                        ww.append(("ten", prev))
                    w2r_done[slot] = W("scalar", lambda e, kp=kp, ds=ds, slot=slot: e.dma_start(
                        out=w2rb(slot),
                        in_=w2p_dram[kp * 128:(kp + 1) * 128, ds * 512:(ds + 1) * 512]),
                        waits=ww, inc=(f"dw2{slot}", 16))
                mw = [(f"dz2{slot}", z2_done[slot]), (f"dw2{slot}", w2r_done[slot])]
                if kp == 0:
                    if first_bmm:
                        mw.append(("vec", at("vec")))
                        mw.append(("sca", at("sca")))
                        first_bmm = False
                    if ui >= 1:
                        mw.append(("vec", max(bev_done[(ui - 1, i)] for i in range(4))))
                for T in range(4):
                    r_ = W("tensor", lambda e, T=T, slot=slot, kp=kp: e.matmul(
                        b_psum[T], z2b(slot)[:, T * 128:(T + 1) * 128], w2rb(slot),
                        start=(kp == 0), stop=(kp == 63), skip_group_check=True),
                        waits=mw if T == 0 else [], inc=("ten", 1) if T == 3 else None)
                    if T == 3:
                        bmm_done[(ui, kp)] = r_
            afs = ui % 2
            for T in range(4):
                ew = [("ten", bmm_done[(ui, 63)])]
                if last_add_AT[afs][T] is not None:
                    ew.append(last_add_AT[afs][T])
                bev_done[(ui, T)] = W("vector", lambda e, T=T, afs=afs: e.tensor_copy(
                    afT(afs, T), b_psum[T]),
                    waits=ew, inc=("vec", 1))
            for T in range(4):
                aw = [("vec", bev_done[(ui, T)])]
                if ui == 0 and T == 0:
                    aw.append(("dxcp", xcp_done))
                afslot_adds[afs] = W("gpsimd", lambda e, T=T, tau=tau, ds=ds, afs=afs: e.dma_start(
                    out=out[tau * 512 + T * 128: tau * 512 + T * 128 + 128,
                            ds * 512:(ds + 1) * 512],
                    in_=afT(afs, T), accum_op=ALU.add),
                    waits=aw, inc=(f"dac{afs}", 16))
                last_add_AT[afs][T] = (f"dac{afs}", afslot_adds[afs])
        p6_adds = (("dac0", at("dac0")), ("dac1", at("dac1")))

        PD = 8
        wo3 = lambda dslot: A3[:, dslot * 8192:(dslot + 1) * 8192]
        og4 = lambda slot: A4[:, slot * 512:(slot + 1) * 512]
        og_done = [0] * PD
        wo_cnt = {}
        pmm_done = {}
        pev_done = {}
        p5_units = [(ds, tcg) for ds in range(4) for tcg in range(2)]
        last_av_ten = av_read[total_pairs - 1]
        for ds0 in (0, 1):
            wo_cnt[ds0] = W("gpsimd", lambda e, ds0=ds0: e.dma_start(
                out=wo3(ds0).rearrange("p (hc d) -> p hc d", hc=16),
                in_=wout[:, ds0 * 512:(ds0 + 1) * 512]
                .rearrange("(hc p) d -> p hc d", p=128)),
                waits=[("ten", last_av_ten)], inc=(f"dwo{ds0}", 16))
        first_p5 = True
        for uix, (ds, tcg) in enumerate(p5_units):
            dslot = ds % 2
            if tcg == 0 and ds >= 2:
                wo_cnt[ds] = W("gpsimd", lambda e, ds=ds, dslot=dslot: e.dma_start(
                    out=wo3(dslot).rearrange("p (hc d) -> p hc d", hc=16),
                    in_=wout[:, ds * 512:(ds + 1) * 512]
                    .rearrange("(hc p) d -> p hc d", p=128)),
                    waits=[("ten", pmm_done[((ds - 2) * 2 + 1, 15)])],
                    inc=(f"dwo{dslot}", 16))
            for hc in range(16):
                slot = hc % PD
                prev = pmm_done.get((uix, hc - PD))
                if prev is None and uix > 0 and hc < PD:
                    prev = pmm_done.get((uix - 1, 16 - PD + hc))
                ogw = [("cc2", 1)] if first_p5 else []
                if prev:
                    ogw.append(("ten", prev))
                og_done[slot] = W("sync", lambda e, hc=hc, tcg=tcg, slot=slot: e.dma_start(
                    out=og4(slot),
                    in_=a2a_out[(hc % 8) * 256 + (hc // 8) * 128:
                                (hc % 8) * 256 + (hc // 8) * 128 + 128,
                                tcg * 512:(tcg + 1) * 512]),
                    waits=ogw, inc=(f"dog4{slot}", 16))
                first_p5 = False
                mw = [(f"dog4{slot}", og_done[slot])]
                if hc == 0:
                    mw.append((f"dwo{dslot}", wo_cnt[ds]))
                    if uix == 0:
                        mw.append(("vec", max(bev_done[(7, i)] for i in range(4))))
                    else:
                        mw.append(("vec", max(pev_done[(uix - 1, i)] for i in range(4))))
                for T in range(4):
                    r_ = W("tensor", lambda e, T=T, slot=slot, hc=hc, dslot=dslot: e.matmul(
                        b_psum[T], og4(slot)[:, T * 128:(T + 1) * 128],
                        wo3(dslot)[:, hc * 512:(hc + 1) * 512],
                        start=(hc == 0), stop=(hc == 15), skip_group_check=True),
                        waits=mw if T == 0 else [], inc=("ten", 1) if T == 3 else None)
                    if T == 3:
                        pmm_done[(uix, hc)] = r_
            afs = uix % 2
            for T in range(4):
                ew = [("ten", pmm_done[(uix, 15)])]
                if last_add_AT[afs][T] is not None:
                    ew.append(last_add_AT[afs][T])
                if uix < 2:
                    ew.extend(p6_adds)
                pev_done[(uix, T)] = W("vector", lambda e, T=T, afs=afs: e.tensor_copy(
                    afT(afs, T), b_psum[T]),
                    waits=ew, inc=("vec", 1))
            for T in range(4):
                aw = [("vec", pev_done[(uix, T)])]
                afslot_adds[afs] = W("gpsimd", lambda e, T=T, tcg=tcg, ds=ds, afs=afs: e.dma_start(
                    out=out[tcg * 512 + T * 128: tcg * 512 + T * 128 + 128,
                            ds * 512:(ds + 1) * 512],
                    in_=afT(afs, T), accum_op=ALU.add),
                    waits=aw, inc=(f"dac{afs}", 16))
                last_add_AT[afs][T] = (f"dac{afs}", afslot_adds[afs])

        W("gpsimd", lambda e: e.nop(),
          waits=[("dac0", at("dac0")), ("dac1", at("dac1")), ("dxcp", xcp_done)])
        W("sync", lambda e: e.nop(),
          waits=[("dac0", at("dac0")), ("dac1", at("dac1"))])

        with nc.Block() as block:
            def mk(ename):
                def emit(eng):
                    lastw = {}
                    for fn, waits, inc in sch.q[ename]:
                        for s_, v in waits:
                            if v > 0 and lastw.get(s_, -1) < v:
                                eng.wait_ge(sems[s_], v)
                                lastw[s_] = v
                        ins = fn(eng)
                        if inc is not None:
                            ins.then_inc(sems[inc[0]], inc[1])
                return emit
            block.tensor(mk("tensor"))
            block.scalar(mk("scalar"))
            block.vector(mk("vector"))
            block.gpsimd(mk("gpsimd"))
            block.sync(mk("sync"))

    return nc


def make_inputs(x, w_qkv, w_gates, w_out, W1, W2, W3):
    x = np.ascontiguousarray(np.asarray(x, np.float32).reshape(S, D))
    w_qkv = np.asarray(w_qkv, np.float32)
    w_gates = np.asarray(w_gates, np.float32)
    w_out = np.ascontiguousarray(np.asarray(w_out, np.float32))
    w1t = np.ascontiguousarray(np.asarray(W1, np.float32).T)
    w3t = np.ascontiguousarray(np.asarray(W3, np.float32).T)
    w2t = np.asarray(W2, np.float32).T
    w2tp = np.ascontiguousarray(w2t.reshape(64, 128, D).transpose(1, 0, 2).reshape(FF, D))
    cst = build_consts()
    fcv = np.zeros((128, 4), np.float32)
    fcv[:, 0] = EPS
    fcv[:, 1] = -1.0
    fcv[:, 2] = FWHT_SCALE * 0.5
    fcv[:, 3] = 1.0
    in_maps = []
    for c in range(NCORE):
        cols = []
        for kind in range(3):
            for s_ in range(2):
                h = c + 8 * s_
                cols.append(np.arange(kind * 2048 + h * 128, kind * 2048 + h * 128 + 128))
        cols = np.concatenate(cols)
        in_maps.append({
            "x": np.ascontiguousarray(x[c * TOWN:(c + 1) * TOWN]),
            "wqkv": np.ascontiguousarray(w_qkv[:, cols]),
            "wg": np.ascontiguousarray(w_gates[:, [c, c + 8]]),
            "wout": w_out,
            "w1t": w1t,
            "w3t": w3t,
            "w2tp": w2tp,
            "cst": cst,
            "fc": fcv,
        })
    return in_maps


def kernel(L0_attn_x, L0_attn_w_qkv, L0_attn_w_gates, L0_attn_w_out,
           L0_ffn_W1, L0_ffn_W2, L0_ffn_W3):
    in_maps = make_inputs(L0_attn_x, L0_attn_w_qkv, L0_attn_w_gates,
                          L0_attn_w_out, L0_ffn_W1, L0_ffn_W2, L0_ffn_W3)
    nc = build_nc()
    res = run_bass_kernel_spmd(nc, in_maps, core_ids=list(range(NCORE)))
    out = np.concatenate([np.asarray(res.results[c]["out"], np.float32)
                          for c in range(NCORE)], axis=0)
    return out.reshape(1, S, D)


if __name__ == "__main__":
    import time
    t0 = time.time()
    build_nc(debug=False)
    print(f"build ok in {time.time() - t0:.1f}s")
